# version 01
# baseline (speedup 1.0000x reference)
import numpy as np
import ml_dtypes
import concourse.bass as bass
import concourse.mybir as mybir
from concourse.bass_utils import run_bass_kernel_spmd
from contextlib import ExitStack

F32 = mybir.dt.float32
BF16 = mybir.dt.bfloat16
I32 = mybir.dt.int32
AF = mybir.ActivationFunctionType
ALU = mybir.AluOpType
AX = mybir.AxisListType
BF = ml_dtypes.bfloat16

D = 2048
T = 2048
SEQ = 8192
G4 = [[0, 1, 2, 3], [4, 5, 6, 7]]
G8 = [[0, 1, 2, 3, 4, 5, 6, 7]]
EXP_C = 16.0
SCALE = float(1.0 / np.sqrt(128.0))


def span_loc(g0):
    p = g0 // 1024
    j = min(p, 7 - p)
    col0 = 0 if p <= 3 else 1024
    return j, col0 + (g0 % 1024)


class P:
    def __init__(self):
        self.nc = bass.Bass("TRN2", target_bir_lowering=False)
        self.ctx = ExitStack()
        self.block = None
        self._gsem = {}
    def guard(self, eng, instr):
        key = id(eng)
        if key not in self._gsem:
            self._gsem[key] = [self.ctx.enter_context(self.nc.semaphore(f"g{len(self._gsem)}")), 0]
        g = self._gsem[key]
        g[1] += 1
        instr.then_inc(g[0], 1)
        eng.wait_ge(g[0], g[1])
        return instr
    def start(self):
        self.block = self.ctx.enter_context(self.nc.Block())
        return self.block
    def sem(self, name):
        return self.ctx.enter_context(self.nc.semaphore(name))
    def sbuf(self, name, shape, dtype):
        return self.ctx.enter_context(self.nc.sbuf_tensor(name, shape, dtype))
    def psum(self, name, shape, dtype=F32):
        return self.ctx.enter_context(self.nc.psum_tensor(name, shape, dtype))
    def close(self):
        self.ctx.close()


def build_ln(debug=False):
    p = P()
    nc = p.nc
    xin = nc.dram_tensor("x_in", [T, D], F32, kind="ExternalInput")
    ident = nc.dram_tensor("ident_bf", [128, 128], BF16, kind="ExternalInput")
    xn_full = nc.dram_tensor("xn_full", [4 * D, T], BF16, kind="ExternalOutput")
    xn_loc = nc.dram_tensor("xn_loc", [D, T], BF16, kind="ExternalOutput")
    xtd = nc.dram_tensor("xn_tok_dbg", [128, 16 * D], BF16, kind="ExternalOutput") if debug else None
    cc_in = nc.dram_tensor("cc_xn_in", [D, T], BF16)
    cc_out = nc.dram_tensor("cc_xn_out", [4 * D, T], BF16)

    xt = [p.sbuf(f"xt{k}", [128, D], F32) for k in range(2)]
    xn_tok = p.sbuf("xn_tok", [128, 16 * D], BF16)
    sq = p.sbuf("sq", [128, D], F32)
    msq = p.sbuf("msq", [128, 1], F32)
    sums = p.sbuf("sums", [128, 16], F32)
    mean = p.sbuf("mean", [128, 16], F32)
    sqa = p.sbuf("sqa", [128, 16], F32)
    var = p.sbuf("var", [128, 16], F32)
    std = p.sbuf("std", [128, 16], F32)
    rstd = p.sbuf("rstd", [128, 16], F32)
    eps = p.sbuf("eps", [128, 1], F32)
    idb = p.sbuf("idb", [128, 128], BF16)
    fmst = [p.sbuf(f"fmst{k}", [128, T], BF16) for k in range(2)]
    pst = [p.psum(f"pst{k}", [128, 128], BF16) for k in range(2)]

    s_ld = [p.sem("s_ld0"), p.sem("s_ld1")]; s_ldc = p.sem("s_ldc")
    s_sq = p.sem("s_sq"); s_var = p.sem("s_var")
    s_std = p.sem("s_std"); s_xn = p.sem("s_xn"); s_tp = p.sem("s_tp")
    s_ev = p.sem("s_ev"); s_fm = [p.sem("s_fm0"), p.sem("s_fm1")]; s_cc = p.sem("s_cc")
    s_done = p.sem("s_done")
    p.start()

    @p.block.sync
    def _(sync):
        sync.dma_start(out=idb[:], in_=ident.ap()).then_inc(s_ldc, 16)
        for i in range(16):
            if i >= 2:
                sync.wait_ge(s_sq, i - 1)
                sync.wait_ge(s_xn, i - 1)
            sync.dma_start(out=xt[i % 2][:], in_=xin.ap()[i * 128:(i + 1) * 128, :]).then_inc(s_ld[i % 2], 16)
        for dc in range(16):
            sync.wait_ge(s_ev, 16 * (dc + 1))
            sync.dma_start(out=cc_in.ap()[dc * 128:(dc + 1) * 128, :], in_=fmst[dc % 2][:]).then_inc(s_fm[dc % 2], 16)
            sync.dma_start(out=xn_loc.ap()[dc * 128:(dc + 1) * 128, :], in_=fmst[dc % 2][:]).then_inc(s_fm[dc % 2], 16)
        if debug:
            sync.dma_start(out=xtd.ap(), in_=xn_tok[:]).then_inc(s_done, 16)
        sync.wait_ge(s_cc, 1)
        sync.dma_start(out=xn_full.ap(), in_=cc_out.ap()).then_inc(s_done, 16)
        sync.wait_ge(s_done, 16 * (2 if debug else 1))

    @p.block.scalar
    def _(sc):
        for i in range(16):
            sc.wait_ge(s_ld[i % 2], 16 * (i // 2 + 1))
            sc.activation(sq[:], xt[i % 2][:], AF.Square,
                          accum_out=sqa[:, i:i + 1]).then_inc(s_sq, 1)
            sc.wait_ge(s_var, i + 1)
            sc.activation(std[:, i:i + 1], var[:, i:i + 1], AF.Sqrt,
                          bias=eps[:], scale=1.0).then_inc(s_std, 1)

    @p.block.vector
    def _(v):
        v.memset(eps[:], 1e-5)
        for i in range(16):
            v.wait_ge(s_ld[i % 2], 16 * (i // 2 + 1))
            p.guard(v, v.reduce_sum(sums[:, i:i + 1], xt[i % 2][:], axis=AX.X))
            p.guard(v, v.tensor_scalar_mul(mean[:, i:i + 1], sums[:, i:i + 1], 1.0 / D))
            v.wait_ge(s_sq, i + 1)
            p.guard(v, v.tensor_scalar_mul(var[:, i:i + 1], sqa[:, i:i + 1], 1.0 / D))
            p.guard(v, v.tensor_tensor(msq[:], mean[:, i:i + 1], mean[:, i:i + 1], ALU.mult))
            v.tensor_tensor(var[:, i:i + 1], var[:, i:i + 1], msq[:],
                            ALU.subtract).then_inc(s_var, 1)
            v.wait_ge(s_std, i + 1)
            p.guard(v, v.reciprocal(rstd[:, i:i + 1], std[:, i:i + 1]))
            v.tensor_scalar(xn_tok[:, i * D:(i + 1) * D], xt[i % 2][:],
                            mean[:, i:i + 1], rstd[:, i:i + 1],
                            ALU.subtract, ALU.mult).then_inc(s_xn, 1)
        for dc in range(16):
            if dc >= 2:
                v.wait_ge(s_fm[dc % 2], 32 * (dc // 2))
            for i in range(16):
                idx = dc * 16 + i
                v.wait_ge(s_tp, idx + 1)
                v.tensor_copy(fmst[dc % 2][:, i * 128:(i + 1) * 128],
                              pst[idx % 2][:]).then_inc(s_ev, 1)

    @p.block.tensor
    def _(t):
        t.wait_ge(s_ldc, 16)
        t.wait_ge(s_xn, 16)
        for dc in range(16):
            for i in range(16):
                idx = dc * 16 + i
                if idx >= 2:
                    t.wait_ge(s_ev, idx - 1)
                t.transpose(pst[idx % 2][:], xn_tok[:, i * D + dc * 128: i * D + dc * 128 + 128],
                            idb[:]).then_inc(s_tp, 1)

    @p.block.gpsimd
    def _(g):
        g.wait_ge(s_fm[0], 32 * 8)
        g.wait_ge(s_fm[1], 32 * 8)
        g.collective_compute("AllGather", ALU.bypass, replica_groups=G4,
                             ins=[cc_in.ap().opt()], outs=[cc_out.ap().opt()]).then_inc(s_cc)

    p.close()
    return p.nc


def stage_inputs(inputs):
    f = np.float32
    x = np.asarray(inputs["L0_attn_x"], f)
    maps = []
    ident_bf = np.eye(128, dtype=BF)
    dm = np.zeros((4, 128, 512), f)
    for dj in range(4):
        m = np.zeros((128, 512), f)
        m[:, dj * 128:(dj + 1) * 128] = np.triu(np.ones((128, 128), f))
        m[:, (dj + 1) * 128:] = 1.0
        dm[dj] = m
    diag_masks = np.ascontiguousarray(dm.reshape(512, 512)).astype(BF)
    ki_ = np.arange(128)[:, None]
    qi_ = np.arange(512)[None, :]
    half = 64
    theta = 1.0 / (10000.0 ** (np.arange(half, dtype=np.float64) / half))
    mm = np.arange(SEQ, dtype=np.float64)
    freqs = np.outer(mm, theta)
    cos_fm = np.ascontiguousarray(np.cos(freqs).T.astype(f)).astype(BF)
    sin_fm = np.ascontiguousarray(np.sin(freqs).T.astype(f)).astype(BF)
    ki = np.arange(128, dtype=f)

    for c in range(8):
        b, j = c // 4, c % 4
        H4 = slice(4 * j * 128, (4 * j + 4) * 128)
        d = {}
        d["x_in"] = np.ascontiguousarray(
            np.concatenate([x[b, j * 1024:(j + 1) * 1024],
                            x[b, (7 - j) * 1024:(8 - j) * 1024]], axis=0))
        qw = np.asarray(inputs["L0_attn_q_w"], f)
        kw = np.asarray(inputs["L0_attn_k_w"], f)
        vw = np.asarray(inputs["L0_attn_v_w"], f)
        d["l0_qkvT"] = np.ascontiguousarray(
            np.concatenate([qw[H4].T, kw[H4].T, vw[H4].T], axis=1))
        d["l0_owT"] = np.ascontiguousarray(np.asarray(inputs["L0_attn_o_w"], f).T)
        d["l0_gate_wT"] = np.ascontiguousarray(np.asarray(inputs["L0_attn_gate_w"], f).T)
        d["l0_gate_b"] = np.ascontiguousarray(np.asarray(inputs["L0_attn_gate_b"], f).reshape(128, 1))
        d["l0_router_w"] = np.ascontiguousarray(np.asarray(inputs["L0_ffn_router_w"], f))
        d["l0_router_b"] = np.ascontiguousarray(np.asarray(inputs["L0_ffn_router_b"], f).reshape(1, 16))
        d["l0_expert_w"] = np.ascontiguousarray(np.asarray(inputs["L0_ffn_expert_w"], f))
        d["l0_expert_b"] = np.ascontiguousarray(
            np.asarray(inputs["L0_ffn_expert_b"], f).reshape(16, 128).T)
        d["l1_qkvT"] = np.ascontiguousarray(np.concatenate(
            [np.asarray(inputs["L1_attn_Wq"], f)[:, H4],
             np.asarray(inputs["L1_attn_Wk"], f)[:, j * 128:(j + 1) * 128],
             np.asarray(inputs["L1_attn_Wv"], f)[:, j * 128:(j + 1) * 128]], axis=1))
        d["l1_wo"] = np.ascontiguousarray(np.asarray(inputs["L1_attn_Wo"], f))
        d["l1_wgT"] = np.ascontiguousarray(np.asarray(inputs["L1_ffn_Wg"], f).T)
        d["l1_w1T"] = np.ascontiguousarray(np.asarray(inputs["L1_ffn_W1"], f).T)
        d["l1_w3T"] = np.ascontiguousarray(np.asarray(inputs["L1_ffn_W3"], f).T)
        d["l1_w2T"] = np.ascontiguousarray(np.asarray(inputs["L1_ffn_W2"], f).T)
        A1 = np.asarray(inputs["L1_ffn_A1"], f); A3 = np.asarray(inputs["L1_ffn_A3"], f)
        A2 = np.asarray(inputs["L1_ffn_A2"], f)
        B1 = np.asarray(inputs["L1_ffn_B1"], f); B3 = np.asarray(inputs["L1_ffn_B3"], f)
        B2 = np.asarray(inputs["L1_ffn_B2"], f)
        d["l1_a1T"] = np.ascontiguousarray(A1.transpose(2, 0, 1).reshape(2048, 256))
        d["l1_a3T"] = np.ascontiguousarray(A3.transpose(2, 0, 1).reshape(2048, 256))
        d["l1_a2T"] = np.ascontiguousarray(A2.transpose(2, 0, 1).reshape(7168, 256))
        d["l1_b1T"] = np.ascontiguousarray(B1.transpose(0, 2, 1).reshape(256, 7168))
        d["l1_b3T"] = np.ascontiguousarray(B3.transpose(0, 2, 1).reshape(256, 7168))
        d["l1_b2T"] = np.ascontiguousarray(B2.transpose(0, 2, 1).reshape(256, 2048))
        d["ident_bf"] = ident_bf
        d["ident_f32"] = np.eye(128, dtype=f)
        d["diag_masks"] = diag_masks
        d["cos_fm"] = cos_fm
        d["sin_fm"] = sin_fm
        hglob = np.arange(4 * j, 4 * j + 4, dtype=f)
        slopes = (2.0 ** (-8.0 * (hglob + 1.0) / 16.0)).astype(np.float64)
        scl = float(1.0 / np.sqrt(128.0))
        kvals = np.arange(SEQ, dtype=np.float64)
        def hilo(v):
            hi = v.astype(BF).astype(np.float64)
            lo = (v - hi).astype(BF)
            return hi.astype(BF), lo
        al_l = np.zeros((4, 4 * 64 * 128), BF)
        al_r = np.zeros((4, 4 * 16 * 512), BF)
        for hl in range(4):
            kv = slopes[hl] * kvals / scl
            khi, klo = hilo(kv)
            qv = -slopes[hl] * kvals / scl
            qhi, qlo = hilo(qv)
            al_l[0, hl * 8192:(hl + 1) * 8192] = BF(1.0)
            al_l[1, hl * 8192:(hl + 1) * 8192] = BF(1.0)
            al_l[2, hl * 8192:(hl + 1) * 8192] = khi
            al_l[3, hl * 8192:(hl + 1) * 8192] = klo
            al_r[0, hl * 8192:(hl + 1) * 8192] = qhi
            al_r[1, hl * 8192:(hl + 1) * 8192] = qlo
            al_r[2, hl * 8192:(hl + 1) * 8192] = BF(1.0)
            al_r[3, hl * 8192:(hl + 1) * 8192] = BF(1.0)
        d["alibi_lhsT"] = np.ascontiguousarray(al_l)
        d["alibi_rhs"] = np.ascontiguousarray(al_r)
        dm1 = np.zeros((4, 4, 128, 512), np.float64)
        for hl in range(4):
            for dj in range(4):
                delta = dj * 128 + ki_ - qi_
                dm1[hl, dj] = np.where(delta <= 0, np.exp(slopes[hl] * delta), 0.0)
        d["diag_masks1"] = np.ascontiguousarray(dm1.reshape(2048, 512).astype(np.float32)).astype(BF)
        gidx_all = np.zeros(16384, f)
        for r in range(8):
            br, jr = r // 4, r % 4
            tl = np.arange(2048)
            pch = np.where(tl < 1024, jr, 7 - jr)
            w = tl % 1024
            gidx_all[r * 2048:(r + 1) * 2048] = br * 8192 + pch * 1024 + w
        gi = gidx_all.reshape(128, 128)
        d["iota_ag_exp"] = np.ascontiguousarray(
            np.repeat(gi[:, :, None], 16, axis=2).reshape(128, 2048))
        gl = gidx_all[c * 2048:(c + 1) * 2048].reshape(16, 128).T
        d["iota_loc_exp"] = np.ascontiguousarray(
            np.repeat(gl[:, :, None], 16, axis=2).reshape(128, 256))
        d["av_off"] = np.array([[j * 512 * 2048]], dtype=np.int32)
        maps.append(d)
    return maps


def filter_map(maps, nc):
    names = set()
    for alloc in nc.m.functions[0].allocations:
        if isinstance(alloc, mybir.MemoryLocationSet) and alloc.kind == "ExternalInput":
            names.add(alloc.memorylocations[0].name)
    return [{k: v for k, v in m.items() if k in names} for m in maps]


def build_qkv(L):
    p = P()
    nc = p.nc
    NE = 12 if L == 0 else 6
    WE = NE * 128
    KE = 512 if L == 0 else 128
    NHV = 4 if L == 0 else 1
    xnf = nc.dram_tensor("xn_full", [4 * D, T], BF16, kind="ExternalInput")
    wT = nc.dram_tensor("l0_qkvT" if L == 0 else "l1_qkvT", [D, WE], F32, kind="ExternalInput")
    q_fm = nc.dram_tensor("q_fm", [512, SEQ], BF16, kind="ExternalOutput")
    k_fm = nc.dram_tensor("k_fm", [KE, SEQ], BF16, kind="ExternalOutput")
    v_blk = nc.dram_tensor("v_blk", [NHV * 64 * 128, 128], BF16, kind="ExternalOutput")

    wsb = p.sbuf("wsb", [128, 16 * WE], BF16)
    xnt = [p.sbuf(f"xnt{i}", [128, 16 * 512], BF16) for i in range(2)]
    est = [p.sbuf(f"est{i}", [128, 512], BF16) for i in range(4)]
    pq = [p.psum(f"pq{i}", [128, 512]) for i in range(2)]
    pv = [p.psum(f"pv{i}", [128, 512]) for i in range(2)]
    s_w = p.sem("s_w"); s_x = [p.sem("s_x0"), p.sem("s_x1")]; s_mm = p.sem("s_mm")
    s_ev = p.sem("s_ev"); s_st = [p.sem(f"s_st{i}") for i in range(4)]
    NEC = 8 if L == 0 else 5
    NTS = 4 if L == 0 else 4
    p.start()

    grp_per_tch = NEC + NTS
    st_per_tch = NEC + NTS * NHV

    @p.block.gpsimd
    def _(g):
        g.dma_start(out=wsb[:], in_=wT.ap().rearrange("(dc p) e -> p dc e", p=128)).then_inc(s_w, 16)

    @p.block.sync
    def _(sync):
        for tch in range(16):
            g0 = tch * 512
            r, col = span_loc(g0)
            if tch >= 2:
                sync.wait_ge(s_mm, grp_per_tch * (tch - 1))
            sync.dma_start(
                out=xnt[tch % 2][:],
                in_=xnf.ap()[r * D:(r + 1) * D, col:col + 512].rearrange("(dc p) t -> p dc t", p=128),
            ).then_inc(s_x[tch % 2], 16)

    @p.block.tensor
    def _(t):
        t.wait_ge(s_w, 16)
        for tch in range(16):
            t.wait_ge(s_x[tch % 2], 16 * (tch // 2 + 1))
            gi = 0
            for ec in range(NEC):
                pp = pq[ec % 2]
                if tch * grp_per_tch + gi >= 2:
                    t.wait_ge(s_ev, tch * grp_per_tch + gi - 1)
                for dc in range(16):
                    mi = t.matmul(pp[:], wsb[:, dc * WE + ec * 128: dc * WE + ec * 128 + 128],
                                  xnt[tch % 2][:, dc * 512:(dc + 1) * 512],
                                  start=(dc == 0), stop=(dc == 15))
                    if dc == 15:
                        mi.then_inc(s_mm, 1)
                gi += 1
            for ts in range(NTS):
                pp = pv[ts % 2]
                if tch * grp_per_tch + gi >= 2:
                    t.wait_ge(s_ev, tch * grp_per_tch + gi - 1)
                for dc in range(16):
                    mi = t.matmul(pp[:, 0:NHV * 128], xnt[tch % 2][:, dc * 512 + ts * 128: dc * 512 + ts * 128 + 128],
                                  wsb[:, dc * WE + NEC * 128: dc * WE + NEC * 128 + NHV * 128],
                                  start=(dc == 0), stop=(dc == 15))
                    if dc == 15:
                        mi.then_inc(s_mm, 1)
                gi += 1

    grp_is_v = ([False] * NEC + [True] * NTS) * 16
    ndma = [(NHV if g_ else 1) for g_ in grp_is_v]
    slot_cum = [[0], [0], [0], [0]]
    for idx, n in enumerate(ndma):
        for sl in range(4):
            slot_cum[sl].append(slot_cum[sl][-1] + (n if idx % 4 == sl else 0))

    @p.block.vector
    def _(v):
        for tch in range(16):
            gi = 0
            for ec in range(NEC):
                idx = tch * grp_per_tch + gi
                v.wait_ge(s_mm, idx + 1)
                if idx >= 4:
                    v.wait_ge(s_st[idx % 4], 16 * slot_cum[idx % 4][idx - 3])
                v.tensor_copy(est[idx % 4][:], pq[ec % 2][:]).then_inc(s_ev, 1)
                gi += 1
            for ts in range(NTS):
                idx = tch * grp_per_tch + gi
                v.wait_ge(s_mm, idx + 1)
                if idx >= 4:
                    v.wait_ge(s_st[idx % 4], 16 * slot_cum[idx % 4][idx - 3])
                v.tensor_copy(est[idx % 4][:, 0:NHV * 128], pv[ts % 2][:, 0:NHV * 128]).then_inc(s_ev, 1)
                gi += 1

    @p.block.scalar
    def _(sc):
        for tch in range(16):
            g0 = tch * 512
            gi = 0
            for ec in range(NEC):
                idx = tch * grp_per_tch + gi
                sc.wait_ge(s_ev, idx + 1)
                if L == 0:
                    dst = q_fm if ec < 4 else k_fm
                    row = (ec % 4) * 128
                else:
                    dst = q_fm if ec < 4 else k_fm
                    row = (ec * 128) if ec < 4 else 0
                sc.dma_start(out=dst.ap()[row:row + 128, g0:g0 + 512],
                             in_=est[idx % 4][:]).then_inc(s_st[idx % 4], 16)
                gi += 1
            for ts in range(NTS):
                idx = tch * grp_per_tch + gi
                sc.wait_ge(s_ev, idx + 1)
                kc = tch * 4 + ts
                for hl in range(NHV):
                    sc.dma_start(out=v_blk.ap()[(hl * 64 + kc) * 128:(hl * 64 + kc) * 128 + 128, :],
                                 in_=est[idx % 4][:, hl * 128:(hl + 1) * 128]).then_inc(s_st[idx % 4], 16)
                gi += 1
        for sl in range(4):
            sc.wait_ge(s_st[sl], 16 * slot_cum[sl][-1])

    p.close()
    return p.nc


def build_rope():
    p = P()
    nc = p.nc
    q_fm = nc.dram_tensor("q_fm", [512, SEQ], BF16, kind="ExternalInput")
    k_fm = nc.dram_tensor("k_fm", [128, SEQ], BF16, kind="ExternalInput")
    q_ro = nc.dram_tensor("q_ro", [512, SEQ], BF16, kind="ExternalOutput")
    k_ro = nc.dram_tensor("k_ro", [128, SEQ], BF16, kind="ExternalOutput")
    cosd = nc.dram_tensor("cos_fm", [64, SEQ], BF16, kind="ExternalInput")
    sind = nc.dram_tensor("sin_fm", [64, SEQ], BF16, kind="ExternalInput")

    hb = p.sbuf("hb", [128, SEQ], BF16)
    ob = p.sbuf("ob", [128, SEQ], BF16)
    cs = p.sbuf("cs", [128, SEQ], BF16)
    sn = p.sbuf("sn", [128, SEQ], BF16)
    t1 = p.sbuf("t1", [64, SEQ], BF16)
    t2 = p.sbuf("t2", [64, SEQ], BF16)
    s_ld = p.sem("s_ld"); s_ldc = p.sem("s_ldc"); s_o = p.sem("s_o"); s_st = p.sem("s_st")
    p.start()

    @p.block.sync
    def _(sync):
        sync.dma_start(out=cs[0:64, :], in_=cosd.ap()).then_inc(s_ldc, 16)
        sync.dma_start(out=cs[64:128, :], in_=cosd.ap()).then_inc(s_ldc, 16)
        sync.dma_start(out=sn[0:64, :], in_=sind.ap()).then_inc(s_ldc, 16)
        sync.dma_start(out=sn[64:128, :], in_=sind.ap()).then_inc(s_ldc, 16)
        for i in range(5):
            if i >= 1:
                sync.wait_ge(s_o, 2 * i)
            src = q_fm if i < 4 else k_fm
            row = (i % 4) * 128
            sync.dma_start(out=hb[:], in_=src.ap()[row:row + 128, :]).then_inc(s_ld, 16)
            sync.wait_ge(s_o, 2 * (i + 1))
            dst = q_ro if i < 4 else k_ro
            sync.dma_start(out=dst.ap()[row:row + 128, :], in_=ob[:]).then_inc(s_st, 16)
        sync.wait_ge(s_st, 16 * 5)

    @p.block.vector
    def _(v):
        v.wait_ge(s_ldc, 64)
        for i in range(5):
            v.wait_ge(s_ld, 16 * (i + 1))
            if i >= 1:
                v.wait_ge(s_st, 16 * i)
            p.guard(v, v.tensor_tensor(t1[:], hb[0:64, :], cs[0:64, :], ALU.mult))
            p.guard(v, v.tensor_tensor(t2[:], hb[64:128, :], sn[64:128, :], ALU.mult))
            v.tensor_tensor(ob[0:64, :], t1[:], t2[:], ALU.subtract).then_inc(s_o, 1)
            v.wait_ge(s_o, 2 * i + 1)
            p.guard(v, v.tensor_tensor(t1[:], hb[0:64, :], sn[0:64, :], ALU.mult))
            p.guard(v, v.tensor_tensor(t2[:], hb[64:128, :], cs[64:128, :], ALU.mult))
            v.tensor_tensor(ob[64:128, :], t1[:], t2[:], ALU.add).then_inc(s_o, 1)
            v.wait_ge(s_o, 2 * i + 2)

    p.close()
    return p.nc


def attn_schedule():
    sched = []
    for hl in range(4):
        for qb in range(16):
            chunks = [(kc, None) for kc in range(4 * qb)]
            chunks += [(4 * qb + dj, dj) for dj in range(4)]
            groups = [chunks[i:i + 3] for i in range(0, len(chunks), 3)]
            sched.append((hl, qb, groups))
    return sched


def build_attn(L):
    p = P()
    nc = p.nc
    KE = 512 if L == 0 else 128
    NHV = 4 if L == 0 else 1
    q_fm = nc.dram_tensor("q_ro" if L == 1 else "q_fm", [512, SEQ], BF16, kind="ExternalInput")
    k_fm = nc.dram_tensor("k_ro" if L == 1 else "k_fm", [KE, SEQ], BF16, kind="ExternalInput")
    v_blk = nc.dram_tensor("v_blk", [NHV * 64 * 128, 128], BF16, kind="ExternalInput")
    NDM = 4 if L == 0 else 16
    dmm = nc.dram_tensor("diag_masks" if L == 0 else "diag_masks1", [NDM * 128, 512], BF16, kind="ExternalInput")
    if L == 0:
        gwT = nc.dram_tensor("l0_gate_wT", [128, 128], F32, kind="ExternalInput")
        gb = nc.dram_tensor("l0_gate_b", [128, 1], F32, kind="ExternalInput")
    else:
        al_l = nc.dram_tensor("alibi_lhsT", [4, 4 * 64 * 128], BF16, kind="ExternalInput")
        al_r = nc.dram_tensor("alibi_rhs", [4, 4 * 16 * 512], BF16, kind="ExternalInput")
    av_full = nc.dram_tensor("av_full", [4 * D, T], BF16, kind="ExternalOutput")
    cc_in = nc.dram_tensor("cc_av_in", [D, T], BF16)
    cc_out = nc.dram_tensor("cc_av_out", [4 * D, T], BF16)

    q_sb = p.sbuf("q_sb", [128, SEQ], BF16)
    k_sb = p.sbuf("k_sb", [128, SEQ], BF16)
    v_sb = p.sbuf("v_sb", [128, 64 * 128], BF16)
    pt_sb = p.sbuf("pt_sb", [128, 2 * 1536], BF16)
    dm_sb = p.sbuf("dm_sb", [128, NDM * 512], BF16)
    rcp_sb = p.sbuf("rcp_sb", [1, 512], F32)
    rb_sb = p.sbuf("rb_sb", [128, 512], F32)
    avn_sb = [p.sbuf(f"avn{i}", [128, 512], BF16) for i in range(2)]
    negC = p.sbuf("negC", [128, 1], F32)
    sclt = p.sbuf("sclt", [128, 1], F32)
    ones_bf = p.sbuf("ones_bf", [128, 1], BF16)
    ones_f32 = p.sbuf("ones_f32", [1, 128], F32)
    if L == 0:
        gw_sb = p.sbuf("gw_sb", [128, 128], BF16)
        gb_sb = p.sbuf("gb_sb", [128, 1], F32)
        gt_sb = p.sbuf("gt_sb", [128, 512], BF16)
        grb_sb = p.sbuf("grb_sb", [128, 512], F32)
    else:
        all_sb = p.sbuf("all_sb", [4, 4 * 64 * 128], BF16)
        alr_sb = p.sbuf("alr_sb", [4, 4 * 16 * 512], BF16)

    ps_st = [p.psum(f"ps_st{i}", [128, 1536]) for i in range(2)]
    ps_av = p.psum("ps_av", [128, 512])
    ps_den = p.psum("ps_den", [1, 512])

    s_hlc = p.sem("s_hlc")
    s_hq = p.sem("s_hq"); s_hk = p.sem("s_hk"); s_hv = p.sem("s_hv")
    s_gw = p.sem("s_gw")
    s_stm = p.sem("s_stm")
    s_exp = p.sem("s_exp")
    s_msk = p.sem("s_msk")
    s_avr = p.sem("s_avr")
    s_rcp = p.sem("s_rcp"); s_rbm = p.sem("s_rbm"); s_rbc = p.sem("s_rbc")
    s_gmm = p.sem("s_gmm"); s_sig = p.sem("s_sig")
    s_avn = p.sem("s_avn"); s_avst = [p.sem("s_avst0"), p.sem("s_avst1")]
    s_cc = p.sem("s_cc"); s_fin = p.sem("s_fin")

    sched = attn_schedule()
    cumgrp = [0]
    for (_, _, groups) in sched:
        cumgrp.append(cumgrp[-1] + len(groups))
    NQBI = 64
    TOTG = cumgrp[-1]
    def hl_dmas(hl):
        return 3 if (L == 0 or hl == 0) else 1
    cum_hld = [0]
    for hl in range(4):
        cum_hld.append(cum_hld[-1] + hl_dmas(hl))
    p.start()

    @p.block.sync
    def _(sync):
        sync.dma_start(out=dm_sb[:], in_=dmm.ap().rearrange("(dj p) q -> p dj q", p=128)).then_inc(s_hlc, 16)
        if L == 1:
            sync.dma_start(out=all_sb[:], in_=al_l.ap()).then_inc(s_hlc, 16)
            sync.dma_start(out=alr_sb[:], in_=al_r.ap()).then_inc(s_hlc, 16)
        for hl in range(4):
            if hl >= 1:
                sync.wait_ge(s_avn, 16 * hl)
            sync.dma_start(out=q_sb[:], in_=q_fm.ap()[hl * 128:(hl + 1) * 128, :]).then_inc(s_hq, 16)
            if L == 0 or hl == 0:
                sync.dma_start(out=k_sb[:], in_=k_fm.ap()[(hl % NHV) * 128:(hl % NHV) * 128 + 128, :]).then_inc(s_hk, 16)
                sync.dma_start(out=v_sb[:],
                               in_=v_blk.ap()[(hl % NHV) * 64 * 128:((hl % NHV) + 1) * 64 * 128, :]
                               .rearrange("(kc pp) dd -> pp kc dd", pp=128)).then_inc(s_hv, 16)
            for qb in range(16):
                qbi = hl * 16 + qb
                sync.wait_ge(s_avn, qbi + 1)
                r, col = span_loc(qb * 512)
                row0 = r * 512 + hl * 128
                sync.dma_start(out=cc_in.ap()[row0:row0 + 128, col:col + 512],
                               in_=avn_sb[qbi % 2][:]).then_inc(s_avst[qbi % 2], 16)
        sync.wait_ge(s_cc, 1)
        sync.dma_start(out=av_full.ap(), in_=cc_out.ap()).then_inc(s_fin, 16)
        sync.wait_ge(s_fin, 16)

    @p.block.gpsimd
    def _(g):
        if L == 0:
            g.dma_start(out=gw_sb[:], in_=gwT.ap()).then_inc(s_gw, 16)
            g.dma_start(out=gb_sb[:], in_=gb.ap()).then_inc(s_gw, 16)
        g.wait_ge(s_avst[0], 16 * NQBI // 2)
        g.wait_ge(s_avst[1], 16 * NQBI // 2)
        g.collective_compute("AllGather", ALU.bypass, replica_groups=G4,
                             ins=[cc_in.ap().opt()], outs=[cc_out.ap().opt()]).then_inc(s_cc)

    CONST_LD = 16 if L == 0 else 48

    def _emit_av(t, qbi, hl, qb, groups, g_, G0):
        G = G0 + g_
        slot = g_ % 2
        t.wait_ge(s_exp, G + 1)
        grp = groups[g_]
        ndiag = sum(1 for (_, dj) in grp if dj is not None)
        if ndiag:
            last_dj = max(dj for (_, dj) in grp if dj is not None)
            t.wait_ge(s_msk, 4 * qbi + last_dj + 1)
        first_g = (g_ == 0)
        last_g = (g_ == len(groups) - 1)
        for ci, (kc, dj) in enumerate(grp):
            first = first_g and ci == 0
            last = last_g and ci == len(grp) - 1
            t.matmul(ps_av[:], v_sb[:, kc * 128:(kc + 1) * 128],
                     pt_sb[:, slot * 1536 + ci * 512: slot * 1536 + (ci + 1) * 512],
                     start=first, stop=last)
            mi = t.matmul(ps_den[:], ones_bf[:],
                          pt_sb[:, slot * 1536 + ci * 512: slot * 1536 + (ci + 1) * 512],
                          start=first, stop=last)
            if ci == len(grp) - 1:
                mi.then_inc(s_avr, 1)

    @p.block.tensor
    def _(t):
        t.wait_ge(s_hlc, CONST_LD)
        if L == 0:
            t.wait_ge(s_gw, 32)
        for qbi, (hl, qb, groups) in enumerate(sched):
            t.wait_ge(s_hq, 16 * (hl + 1))
            t.wait_ge(s_hk, 16 * ((hl + 1) if L == 0 else 1))
            t.wait_ge(s_hv, 16 * ((hl + 1) if L == 0 else 1))
            if qbi >= 1:
                t.wait_ge(s_rbc, qbi)
                if L == 0:
                    t.wait_ge(s_sig, qbi)
                t.wait_ge(s_avn, qbi)
            G0 = cumgrp[qbi]
            ng = len(groups)
            for g_ in range(ng):
                G = G0 + g_
                slot = g_ % 2
                if g_ >= 2:
                    t.wait_ge(s_exp, G - 1)
                for ci, (kc, dj) in enumerate(groups[g_]):
                    with_alibi = (L == 1 and dj is None)
                    mi = t.matmul(ps_st[slot][:, ci * 512:(ci + 1) * 512],
                                  k_sb[:, kc * 128:(kc + 1) * 128],
                                  q_sb[:, qb * 512:(qb + 1) * 512],
                                  start=True, stop=not with_alibi)
                    if with_alibi:
                        mi = t.matmul(ps_st[slot][:, ci * 512:(ci + 1) * 512],
                                      all_sb[:, hl * 8192 + kc * 128: hl * 8192 + (kc + 1) * 128],
                                      alr_sb[:, hl * 8192 + qb * 512: hl * 8192 + (qb + 1) * 512],
                                      start=False, stop=True)
                    if ci == len(groups[g_]) - 1:
                        mi.then_inc(s_stm, 1)
                if g_ >= 1:
                    _emit_av(t, qbi, hl, qb, groups, g_ - 1, G0)
            _emit_av(t, qbi, hl, qb, groups, ng - 1, G0)
            t.wait_ge(s_rcp, qbi + 1)
            t.matmul(ps_st[0][:, 0:512].bitcast(F32), ones_f32[:], rcp_sb[:],
                     start=True, stop=True).then_inc(s_rbm, 1)
            if L == 0:
                t.matmul(ps_st[1][:, 0:512].bitcast(F32), gw_sb[:],
                         q_sb[:, qb * 512:(qb + 1) * 512], start=True, stop=True).then_inc(s_gmm, 1)

    s_cst = p.sem("s_cst")

    @p.block.scalar
    def _(sc):
        sc.wait_ge(s_cst, 1)
        for qbi, (hl, qb, groups) in enumerate(sched):
            G0 = cumgrp[qbi]
            for g_ in range(len(groups)):
                G = G0 + g_
                slot = g_ % 2
                sc.wait_ge(s_stm, G + 1)
                if G >= 2:
                    sc.wait_ge(s_avr, G - 1)
                n = len(groups[g_])
                sc.activation(pt_sb[:, slot * 1536: slot * 1536 + n * 512],
                              ps_st[slot][:, 0:n * 512], AF.Exp,
                              bias=negC[:], scale=sclt[:]).then_inc(s_exp, 1)
            sc.wait_ge(s_rbm, qbi + 1)
            sc.copy(rb_sb[:], ps_st[0][:, 0:512].bitcast(F32)).then_inc(s_rbc, 1)
            if L == 0:
                sc.wait_ge(s_gmm, qbi + 1)
                sc.activation(gt_sb[:], ps_st[1][:, 0:512].bitcast(F32), AF.Sigmoid,
                              bias=gb_sb[:], scale=1.0).then_inc(s_sig, 1)

    @p.block.vector
    def _(v):
        v.memset(ones_bf[:], 1.0)
        v.memset(ones_f32[:], 1.0)
        v.memset(negC[:], -EXP_C)
        v.memset(sclt[:], SCALE).then_inc(s_cst, 1)
        for qbi, (hl, qb, groups) in enumerate(sched):
            G0 = cumgrp[qbi]
            for g_ in range(len(groups)):
                for ci, (kc, dj) in enumerate(groups[g_]):
                    if dj is None:
                        continue
                    G = G0 + g_
                    slot = g_ % 2
                    v.wait_ge(s_exp, G + 1)
                    sl = pt_sb[:, slot * 1536 + ci * 512: slot * 1536 + (ci + 1) * 512]
                    mi_ = dj if L == 0 else (hl * 4 + dj)
                    v.tensor_tensor(sl, sl, dm_sb[:, mi_ * 512:(mi_ + 1) * 512],
                                    ALU.mult).then_inc(s_msk, 1)
            v.wait_ge(s_avr, cumgrp[qbi + 1])
            v.reciprocal(rcp_sb[:], ps_den[0:1, :]).then_inc(s_rcp, 1)
            v.wait_ge(s_rbc, qbi + 1)
            if qbi >= 2:
                v.wait_ge(s_avst[qbi % 2], 16 * (qbi // 2))
            if L == 0:
                v.wait_ge(s_sig, qbi + 1)
                p.guard(v, v.tensor_tensor(grb_sb[:], gt_sb[:], rb_sb[:], ALU.mult))
                v.tensor_tensor(avn_sb[qbi % 2][:], ps_av[:], grb_sb[:], ALU.mult).then_inc(s_avn, 1)
            else:
                v.tensor_tensor(avn_sb[qbi % 2][:], ps_av[:], rb_sb[:], ALU.mult).then_inc(s_avn, 1)

    p.close()
    return p.nc


def build_ffn0(upto="full"):
    p = P()
    nc = p.nc
    xnl = nc.dram_tensor("xn_loc", [D, T], BF16, kind="ExternalInput")
    rwT = nc.dram_tensor("l0_router_w", [D, 16], F32, kind="ExternalInput")
    rbr = nc.dram_tensor("l0_router_b", [1, 16], F32, kind="ExternalInput")
    ewT = nc.dram_tensor("l0_expert_w", [D, D], F32, kind="ExternalInput")
    ebr = nc.dram_tensor("l0_expert_b", [128, 16], F32, kind="ExternalInput")
    iag = nc.dram_tensor("iota_ag_exp", [128, 2048], F32, kind="ExternalInput")
    ilo = nc.dram_tensor("iota_loc_exp", [128, 256], F32, kind="ExternalInput")
    idf = nc.dram_tensor("ident_f32", [128, 128], F32, kind="ExternalInput")
    g_fm = nc.dram_tensor("g_fm", [D, T], F32, kind="ExternalOutput")
    dbg = nc.dram_tensor("dbg", [128, 256], F32, kind="ExternalOutput") if upto != "full" else None
    dbg2 = nc.dram_tensor("dbg2", [1, 2048], F32, kind="ExternalOutput") if upto != "full" else None
    cc_in = nc.dram_tensor("cc_S_in", [T // 8, 128], F32)
    cc_out = nc.dram_tensor("cc_S_out", [T, 128], F32)
    c_dram = nc.dram_tensor("c_scr", [1, T], F32)

    xn_sb = p.sbuf("xn_sb", [128, 16 * T], BF16)
    ew_sb = p.sbuf("ew_sb", [128, 16 * D], BF16)
    rw_sb = p.sbuf("rw_sb", [128, 16 * 16], BF16)
    rb_sb = p.sbuf("rb_sb", [1, 16], BF16)
    S_loc = p.sbuf("S_loc", [128, 256], F32)
    S_sb = p.sbuf("S_sb", [128, 2048], F32)
    iag_sb = p.sbuf("iag_sb", [128, 2048], F32)
    ilo_sb = p.sbuf("ilo_sb", [128, 256], F32)
    eb_sb = p.sbuf("eb_sb", [128, 16], F32)
    exp_sb = p.sbuf("exp_sb", [128, 16], F32)
    srow = p.sbuf("srow", [128, 8], F32)
    loB = [p.sbuf(f"lo_sb{i}", [1, 16], F32) for i in range(2)]
    hiB = [p.sbuf(f"hi_sb{i}", [1, 16], F32) for i in range(2)]
    vhi_sb = p.sbuf("vhi_sb", [1, 16], F32)
    mid_sb = p.sbuf("mid_sb", [1, 16], F32)
    mide = p.sbuf("mide", [1, 2048], F32)
    cnt_sb = p.sbuf("cnt_sb", [1, 16], F32)
    cge_sb = p.sbuf("cge_sb", [1, 16], I32)
    need_sb = p.sbuf("need_sb", [1, 16], F32)
    ge_sb = p.sbuf("ge_sb", [128, 2048], F32)
    eq_sb = p.sbuf("eq_sb", [128, 2048], F32)
    lt_sb = p.sbuf("lt_sb", [128, 2048], F32)
    part_bf = p.sbuf("part_bf", [128, 16], BF16)
    msk_sb = p.sbuf("msk_sb", [128, 256], F32)
    eqloc_sb = p.sbuf("eqloc_sb", [128, 256], F32)
    ltloc_sb = p.sbuf("ltloc_sb", [128, 256], F32)
    c_tok = p.sbuf("c_tok", [128, 16], F32)
    c_t2 = p.sbuf("c_t2", [16, 128], F32)
    idf_sb = p.sbuf("idf_sb", [128, 128], F32)
    c_row = p.sbuf("c_row", [1, T], F32)
    ones_bf = p.sbuf("ones_bf", [128, 1], BF16)
    ones_f32 = p.sbuf("ones_f32", [1, 128], F32)
    onesr_bf = p.sbuf("onesr_bf", [1, 128], BF16)
    ebias_sb = p.sbuf("ebias_sb", [128, 512], F32)
    y_sb = p.sbuf("y_sb", [128, 512], F32)
    g_sb = [p.sbuf(f"g_sb{i}", [128, 512], F32) for i in range(2)]

    ps_r = p.psum("ps_r", [128, 16])
    ps_b = p.psum("ps_b", [128, 2048])
    ps_e = [p.psum(f"ps_e{i}", [128, 512]) for i in range(2)]

    s_ld = p.sem("s_ld"); s_gld = p.sem("s_gld"); s_rt = p.sem("s_rt"); s_sm = p.sem("s_sm")
    s_sx = p.sem("s_sx"); s_sl = p.sem("s_sl"); s_sst = p.sem("s_sst")
    s_cc = p.sem("s_cc"); s_bt = p.sem("s_bt"); s_bv = p.sem("s_bv")
    s_cdma = p.sem("s_cdma"); s_ct = p.sem("s_ct"); s_em = p.sem("s_em"); s_ee = p.sem("s_ee")
    s_cbm = p.sem("s_cbm"); s_y = p.sem("s_y"); s_g = p.sem("s_g")
    s_gst = [p.sem("s_gst0"), p.sem("s_gst1")]; s_cst0 = p.sem("s_cst0")

    NB = 44
    NTB = 16
    p.start()

    @p.block.sync
    def _(sync):
        sync.dma_start(out=xn_sb[:], in_=xnl.ap().rearrange("(dc pp) t -> pp dc t", pp=128)).then_inc(s_ld, 16)
        sync.dma_start(out=iag_sb[:], in_=iag.ap()).then_inc(s_ld, 16)
        sync.dma_start(out=ilo_sb[:], in_=ilo.ap()).then_inc(s_ld, 16)
        sync.dma_start(out=eb_sb[:], in_=ebr.ap()).then_inc(s_ld, 16)
        sync.dma_start(out=idf_sb[:], in_=idf.ap()).then_inc(s_ld, 16)
        for i in range(16):
            sync.wait_ge(s_sl, i + 1)
            sync.dma_start(out=cc_in.ap().rearrange("r (a e) -> (r a) e", e=16)[i * 128:(i + 1) * 128, :],
                           in_=S_loc[:, i * 16:(i + 1) * 16]).then_inc(s_sst, 16)
        if upto == "softmax":
            sync.wait_ge(s_sst, 256)
            sync.dma_start(out=dbg.ap(), in_=S_loc[:]).then_inc(s_gst, 16)
            sync.wait_ge(s_gst, 16)
            return
        sync.wait_ge(s_cc, 1)
        sync.dma_start(out=S_sb[:], in_=cc_out.ap().rearrange("(pp i8) w -> pp (i8 w)", i8=16)).then_inc(s_ld, 16)
        sync.wait_ge(s_cdma, 2)
        sync.dma_start(out=bass.AP(c_dram, 0, [[128, 16], [1, 128]]), in_=c_t2[:]).then_inc(s_cdma, 16)
        sync.wait_ge(s_cdma, 18)
        sync.dma_start(out=c_row[:], in_=c_dram.ap()).then_inc(s_cdma, 16)
        if upto == "full":
            for ti in range(64):
                sync.wait_ge(s_g, ti + 1)
                ec, tch = ti // 4, ti % 4
                sync.dma_start(out=g_fm.ap()[ec * 128:(ec + 1) * 128, tch * 512:(tch + 1) * 512],
                               in_=g_sb[ti % 2][:]).then_inc(s_gst[ti % 2], 16)
            sync.wait_ge(s_gst[0], 16 * 32)
            sync.wait_ge(s_gst[1], 16 * 32)
        elif upto == "router":
            sync.dma_start(out=dbg.ap(), in_=S_loc[:]).then_inc(s_gst[0], 16)
            sync.dma_start(out=dbg2.ap(), in_=S_sb[0:1, :]).then_inc(s_gst[1], 16)
            sync.wait_ge(s_gst[0], 16)
            sync.wait_ge(s_gst[1], 16)
        else:
            sync.wait_ge(s_cdma, 34)
            sync.dma_start(out=dbg.ap(), in_=S_loc[:]).then_inc(s_gst[0], 16)
            sync.dma_start(out=dbg2.ap()[:, 0:16], in_=vhi_sb[:]).then_inc(s_gst[0], 16)
            sync.dma_start(out=dbg2.ap()[:, 16:32], in_=hiB[NTB % 2][:]).then_inc(s_gst[0], 16)
            sync.dma_start(out=dbg2.ap()[:, 32:48], in_=need_sb[:]).then_inc(s_gst[0], 16)
            sync.dma_start(out=dbg2.ap()[:, 48:2048], in_=c_row[:, 0:2000]).then_inc(s_gst[0], 16)
            sync.wait_ge(s_gst[0], 16 * 5)

    @p.block.gpsimd
    def _(g):
        g.dma_start(out=rw_sb[:], in_=rwT.ap().rearrange("(dc pp) e -> pp dc e", pp=128)).then_inc(s_gld, 16)
        g.dma_start(out=rb_sb[:], in_=rbr.ap()).then_inc(s_gld, 16)
        g.dma_start(out=ew_sb[:], in_=ewT.ap().rearrange("(hc pp) e -> pp hc e", pp=128)).then_inc(s_gld, 16)
        if upto == "softmax":
            return
        g.wait_ge(s_sst, 16 * 16)
        g.collective_compute("AllGather", ALU.bypass, replica_groups=G8,
                             ins=[cc_in.ap().opt()], outs=[cc_out.ap().opt()]).then_inc(s_cc)

    NPHASE = 2 * (NB + 1) + 2 * NTB + 2

    @p.block.tensor
    def _(t):
        t.wait_ge(s_ld, 16 * 5)
        t.wait_ge(s_gld, 16 * 3)
        t.wait_ge(s_cst0, 1)
        for i in range(16):
            if i >= 1:
                t.wait_ge(s_sx, i)
            for dc in range(16):
                t.matmul(ps_r[:], xn_sb[:, dc * T + i * 128: dc * T + i * 128 + 128],
                         rw_sb[:, dc * 16:(dc + 1) * 16], start=(dc == 0), stop=False)
            t.matmul(ps_r[:], onesr_bf[:], rb_sb[:], start=False, stop=True).then_inc(s_rt, 1)
        if upto in ("router", "softmax"):
            return
        t.wait_ge(s_ld, 16 * 6)
        t.wait_ge(s_sx, 16)
        nbv = 0; nbt = 0
        for it in range(NB + 1 + NTB):
            nbv += 1
            t.wait_ge(s_bv, nbv)
            for jj in range(4):
                mi = t.matmul(ps_b[:, jj * 512:(jj + 1) * 512], ones_f32[:], mide[:, jj * 512:(jj + 1) * 512],
                              start=True, stop=True)
            mi.then_inc(s_bt, 1); nbt += 1
            nbv += 1
            t.wait_ge(s_bv, nbv)
            t.matmul(ps_r[0:1, :], ones_bf[:], part_bf[:], start=True, stop=True).then_inc(s_bt, 1); nbt += 1
        for z in range(2):
            nbv += 1
            t.wait_ge(s_bv, nbv)
            t.matmul(ps_b[:, 0:256], ones_f32[:], mide[:, 0:256], start=True, stop=True).then_inc(s_bt, 1)
            nbt += 1
        t.wait_ge(s_cdma, 1)
        t.transpose(ps_e[0][0:16, 0:128], c_tok[:], idf_sb[:]).then_inc(s_ct, 1)
        if upto != "full":
            return
        t.wait_ge(s_cdma, 34)
        for ti in range(64):
            ec, tch = ti // 4, ti % 4
            if ti >= 2:
                t.wait_ge(s_ee, ti - 1)
            for dc in range(16):
                mi = t.matmul(ps_e[ti % 2][:], ew_sb[:, dc * D + ec * 128: dc * D + ec * 128 + 128],
                              xn_sb[:, dc * T + tch * 512: dc * T + (tch + 1) * 512],
                              start=(dc == 0), stop=(dc == 15))
                if dc == 15:
                    mi.then_inc(s_em, 1)
            if ti >= 1:
                t.wait_ge(s_y, ti)
            t.matmul(ps_b[:, 0:512], ones_f32[:], c_row[:, tch * 512:(tch + 1) * 512],
                     start=True, stop=True).then_inc(s_cbm, 1)

    @p.block.scalar
    def _(sc):
        for i in range(16):
            sc.wait_ge(s_rt, i + 1)
            sc.wait_ge(s_sm, 2 * (i + 1))
            sc.activation(exp_sb[:], ps_r[:], AF.Exp, bias=srow[:, 1:2], scale=1.0,
                          accum_out=srow[:, 2:3]).then_inc(s_sx, 1)
        if upto != "full":
            return
        for ti in range(64):
            ec, tch = ti // 4, ti % 4
            sc.wait_ge(s_em, ti + 1)
            sc.activation(ebias_sb[:], ps_e[ti % 2][:], AF.Identity,
                          bias=eb_sb[:, ec:ec + 1], scale=1.0).then_inc(s_ee, 1)
            sc.wait_ge(s_y, ti + 1)
            if ti >= 2:
                sc.wait_ge(s_gst[ti % 2], 16 * (ti // 2))
            sc.activation(g_sb[ti % 2][:], y_sb[:], AF.Silu).then_inc(s_g, 1)

    @p.block.vector
    def _(v):
        v.memset(ones_bf[:], 1.0)
        v.memset(onesr_bf[:], 1.0)
        v.memset(ones_f32[:], 1.0)
        v.memset(loB[0][:], 0.0)
        v.memset(hiB[0][:], 1.0).then_inc(s_cst0, 1)
        for i in range(16):
            v.wait_ge(s_rt, i + 1)
            p.guard(v, v.reduce_max(srow[:, 0:1], ps_r[:], axis=AX.X))
            v.tensor_scalar_mul(srow[:, 1:2], srow[:, 0:1], -1.0).then_inc(s_sm, 2)
            v.wait_ge(s_sx, i + 1)
            p.guard(v, v.reciprocal(srow[:, 3:4], srow[:, 2:3]))
            v.tensor_scalar_mul(S_loc[:, i * 16:(i + 1) * 16], exp_sb[:], srow[:, 3:4]).then_inc(s_sl, 1)
        if upto in ("router", "softmax"):
            return
        nbv = 0; nbt = 0
        mid_bcast_ap = bass.AP(mid_sb.ap().tensor, 0, [[16, 1], [0, 128], [1, 16]])
        for it in range(NB + 1 + NTB):
            tie = it > NB
            pp = it % 2 if it < NB else (it - NB - 1) % 2
            qq = 1 - pp
            if it < NB:
                p.guard(v, v.tensor_add(mid_sb[:], loB[pp][:], hiB[pp][:]))
                p.guard(v, v.tensor_scalar_mul(mid_sb[:], mid_sb[:], 0.5))
            elif it == NB:
                p.guard(v, v.tensor_copy(mid_sb[:], hiB[NB % 2][:]))
            else:
                p.guard(v, v.tensor_add(mid_sb[:], loB[pp][:], hiB[pp][:]))
                p.guard(v, v.tensor_scalar_mul(mid_sb[:], mid_sb[:], 0.5))
            v.tensor_copy(mide.ap().rearrange("o (i e) -> o i e", e=16), mid_bcast_ap).then_inc(s_bv, 1); nbv += 1
            nbt += 1
            v.wait_ge(s_bt, nbt)
            if not tie:
                p.guard(v, v.tensor_tensor(ge_sb[:], S_sb[:], ps_b[:], ALU.is_gt))
                with p.nc.allow_low_precision(reason="counts<=128 exact in bf16"):
                    v.reduce_sum(part_bf[:], ge_sb.ap().rearrange("pp (i e) -> pp e i", e=16),
                                 axis=AX.X).then_inc(s_bv, 1)
            else:
                p.guard(v, v.tensor_tensor(lt_sb[:], iag_sb[:], ps_b[:], ALU.is_lt))
                p.guard(v, v.tensor_tensor(lt_sb[:], lt_sb[:], eq_sb[:], ALU.mult))
                with p.nc.allow_low_precision(reason="counts<=128 exact in bf16"):
                    v.reduce_sum(part_bf[:], lt_sb.ap().rearrange("pp (i e) -> pp e i", e=16),
                                 axis=AX.X).then_inc(s_bv, 1)
            nbv += 1
            nbt += 1
            v.wait_ge(s_bt, nbt)
            p.guard(v, v.tensor_copy(cnt_sb[:], ps_r[0:1, :]))
            if it < NB:
                p.guard(v, v.tensor_scalar(cge_sb[:], cnt_sb[:], 1024.0, None, ALU.is_ge))
                p.guard(v, v.select(loB[qq][:], cge_sb[:], mid_sb[:], loB[pp][:], add_drain=True))
                p.guard(v, v.select(hiB[qq][:], cge_sb[:], hiB[pp][:], mid_sb[:], add_drain=True))
            elif it == NB:
                p.guard(v, v.tensor_scalar(need_sb[:], cnt_sb[:], -1.0, 1024.0, ALU.mult, ALU.add))
                p.guard(v, v.tensor_tensor(eq_sb[:], S_sb[:], ps_b[:], ALU.is_equal))
                p.guard(v, v.tensor_copy(vhi_sb[:], hiB[NB % 2][:]))
                p.guard(v, v.memset(loB[0][:], 0.0))
                p.guard(v, v.memset(hiB[0][:], 16384.0))
            else:
                p.guard(v, v.tensor_tensor(cge_sb[:], cnt_sb[:], need_sb[:], ALU.is_ge))
                p.guard(v, v.select(hiB[qq][:], cge_sb[:], mid_sb[:], hiB[pp][:], add_drain=True))
                p.guard(v, v.select(loB[qq][:], cge_sb[:], loB[pp][:], mid_sb[:], add_drain=True))
        for z in range(2):
            src = vhi_sb if z == 0 else hiB[NTB % 2]
            src_ap = bass.AP(src.ap().tensor, 0, [[16, 1], [0, 16], [1, 16]])
            v.tensor_copy(mide[:, 0:256].rearrange("o (i e) -> o i e", e=16), src_ap).then_inc(s_bv, 1)
            nbv += 1
            nbt += 1
            v.wait_ge(s_bt, nbt)
            if z == 0:
                p.guard(v, v.tensor_tensor(msk_sb[:], S_loc[:], ps_b[:, 0:256], ALU.is_gt))
                p.guard(v, v.tensor_tensor(eqloc_sb[:], S_loc[:], ps_b[:, 0:256], ALU.is_equal))
            else:
                p.guard(v, v.tensor_tensor(ltloc_sb[:], ilo_sb[:], ps_b[:, 0:256], ALU.is_lt))
                p.guard(v, v.tensor_tensor(eqloc_sb[:], eqloc_sb[:], ltloc_sb[:], ALU.mult))
                p.guard(v, v.tensor_tensor(msk_sb[:], msk_sb[:], eqloc_sb[:], ALU.add))
                p.guard(v, v.tensor_tensor(msk_sb[:], msk_sb[:], S_loc[:], ALU.mult))
                v.reduce_sum(c_tok[:], msk_sb.ap().rearrange("pp (i e) -> pp i e", e=16),
                             axis=AX.X).then_inc(s_cdma, 1)
                v.wait_ge(s_ct, 1)
                v.tensor_copy(c_t2[:], ps_e[0][0:16, 0:128]).then_inc(s_cdma, 1)
        if upto == "full":
            for ti in range(64):
                v.wait_ge(s_ee, ti + 1)
                v.wait_ge(s_cbm, ti + 1)
                if ti >= 1:
                    v.wait_ge(s_g, ti)
                v.tensor_tensor(y_sb[:], ebias_sb[:], ps_b[:, 0:512], ALU.mult).then_inc(s_y, 1)

    p.close()
    return p.nc


def build_ep(L):
    p = P()
    nc = p.nc
    avf = nc.dram_tensor("av_full", [4 * D, T], BF16, kind="ExternalInput")
    woT = nc.dram_tensor("l0_owT" if L == 0 else "l1_wo", [D, D], F32, kind="ExternalInput")
    gfm = nc.dram_tensor("g_fm", [D, T], F32, kind="ExternalInput")
    xin = nc.dram_tensor("x_in", [T, D], F32, kind="ExternalInput")
    offt = nc.dram_tensor("av_off", [1, 1], I32, kind="ExternalInput")
    idf = nc.dram_tensor("ident_f32", [128, 128], F32, kind="ExternalInput")
    xout = nc.dram_tensor("x_out", [T, D], F32, kind="ExternalOutput")

    av_sb = p.sbuf("av_sb", [128, 16 * T], BF16)
    wo_sb = p.sbuf("wo_sb", [128, 16 * D], BF16)
    gt_sb = [p.sbuf(f"gt{i}", [128, 512], F32) for i in range(2)]
    sg_sb = p.sbuf("sg_sb", [128, 512], F32)
    sgT_sb = p.sbuf("sgT_sb", [128, 4 * D], F32)
    xt_sb = [p.sbuf(f"xts{i}", [128, D], F32) for i in range(2)]
    xo_sb = [p.sbuf(f"xo{i}", [128, D], F32) for i in range(2)]
    idf_sb = p.sbuf("idf_sb", [128, 128], F32)
    off_sb = p.sbuf("off_sb", [1, 1], I32)

    ps_o = [p.psum(f"ps_o{i}", [128, 512]) for i in range(2)]
    ps_t = [p.psum(f"ps_t{i}", [128, 128]) for i in range(2)]

    s_ld = p.sem("s_ld"); s_gl2 = p.sem("s_gl2"); s_av = p.sem("s_av")
    s_om = p.sem("s_om"); s_sg = p.sem("s_sg"); s_tp = p.sem("s_tp")
    s_te = p.sem("s_te"); s_xld = [p.sem("s_xld0"), p.sem("s_xld1")]; s_xo = p.sem("s_xo")
    s_st = [p.sem("s_stp0"), p.sem("s_stp1")]; s_gld = [p.sem("s_gld0"), p.sem("s_gld1")]
    p.start()

    @p.block.gpsimd
    def _(g):
        g.dma_start(out=off_sb[:], in_=offt.ap()).then_inc(s_gl2, 16)
        g.dma_start(out=wo_sb[:], in_=woT.ap().rearrange("(dc pp) e -> pp dc e", pp=128)).then_inc(s_gl2, 16)
        g.wait_ge(s_gl2, 32)
        with g.register("goff") as goff, g.register("gr2") as gr2:
            g.reg_load(goff, off_sb[:1, :1])
            for hc in range(16):
                r, hl = hc // 4, hc % 4
                fixed = (r * 2048 + hl * 128) * 2048
                g.reg_add(gr2, goff, fixed)
                g.dma_start(out=av_sb[:, hc * T:(hc + 1) * T],
                            in_=bass.AP(avf, gr2, [[2048, 128], [1, 2048]])).then_inc(s_av, 16)

    @p.block.sync
    def _(sync):
        sync.dma_start(out=idf_sb[:], in_=idf.ap()).then_inc(s_ld, 16)
        for tch in range(4):
            for ec in range(16):
                ti = tch * 16 + ec
                if ti >= 2:
                    sync.wait_ge(s_sg, ti - 1)
                sync.dma_start(out=gt_sb[ti % 2][:],
                               in_=gfm.ap()[ec * 128:(ec + 1) * 128, tch * 512:(tch + 1) * 512]).then_inc(s_gld[ti % 2], 16)
            for sub in range(4):
                si = tch * 4 + sub
                if si >= 2:
                    sync.wait_ge(s_xo, si - 1)
                sync.dma_start(out=xt_sb[si % 2][:],
                               in_=xin.ap()[si * 128:(si + 1) * 128, :]).then_inc(s_xld[si % 2], 16)
                sync.wait_ge(s_xo, si + 1)
                sync.dma_start(out=xout.ap()[si * 128:(si + 1) * 128, :],
                               in_=xo_sb[si % 2][:]).then_inc(s_st[si % 2], 16)
        sync.wait_ge(s_st[0], 16 * 8)
        sync.wait_ge(s_st[1], 16 * 8)

    @p.block.tensor
    def _(t):
        t.wait_ge(s_ld, 16)
        t.wait_ge(s_gl2, 32)
        t.wait_ge(s_av, 16 * 16)
        for tch in range(4):
            for ec in range(16):
                ti = tch * 16 + ec
                if ti >= 2:
                    t.wait_ge(s_sg, ti - 1)
                for dc in range(16):
                    mi = t.matmul(ps_o[ti % 2][:], wo_sb[:, dc * D + ec * 128: dc * D + ec * 128 + 128],
                                  av_sb[:, dc * T + tch * 512: dc * T + (tch + 1) * 512],
                                  start=(dc == 0), stop=(dc == 15))
                    if dc == 15:
                        mi.then_inc(s_om, 1)
                t.wait_ge(s_sg, ti + 1)
                for sub in range(4):
                    tpi = ti * 4 + sub
                    if tpi >= 2:
                        t.wait_ge(s_te, tpi - 1)
                    t.transpose(ps_t[tpi % 2][:], sg_sb[:, sub * 128:(sub + 1) * 128],
                                idf_sb[:]).then_inc(s_tp, 1)

    @p.block.vector
    def _(v):
        for tch in range(4):
            for ec in range(16):
                ti = tch * 16 + ec
                v.wait_ge(s_om, ti + 1)
                v.wait_ge(s_gld[ti % 2], 16 * (ti // 2 + 1))
                if ti >= 1:
                    v.wait_ge(s_tp, 4 * ti)
                if ec == 0 and tch >= 1:
                    v.wait_ge(s_xo, 4 * tch)
                v.tensor_tensor(sg_sb[:], ps_o[ti % 2][:], gt_sb[ti % 2][:], ALU.mult).then_inc(s_sg, 1)
                for sub in range(4):
                    tpi = ti * 4 + sub
                    v.wait_ge(s_tp, tpi + 1)
                    v.tensor_copy(sgT_sb[:, sub * D + ec * 128: sub * D + ec * 128 + 128],
                                  ps_t[tpi % 2][:]).then_inc(s_te, 1)
            for sub in range(4):
                si = tch * 4 + sub
                v.wait_ge(s_xld[si % 2], 16 * (si // 2 + 1))
                if si >= 2:
                    v.wait_ge(s_st[si % 2], 16 * (si // 2))
                v.tensor_tensor(xo_sb[si % 2][:], xt_sb[si % 2][:], sgT_sb[:, sub * D:(sub + 1) * D],
                                ALU.add).then_inc(s_xo, 1)

    p.close()
    return p.nc


def build_ffn1():
    p = P()
    nc = p.nc
    xnl = nc.dram_tensor("xn_loc", [D, T], BF16, kind="ExternalInput")
    wgT = nc.dram_tensor("l1_wgT", [D, 8], F32, kind="ExternalInput")
    w1T = nc.dram_tensor("l1_w1T", [D, 7168], F32, kind="ExternalInput")
    w3T = nc.dram_tensor("l1_w3T", [D, 7168], F32, kind="ExternalInput")
    w2T = nc.dram_tensor("l1_w2T", [7168, D], F32, kind="ExternalInput")
    a1T = nc.dram_tensor("l1_a1T", [D, 256], F32, kind="ExternalInput")
    a3T = nc.dram_tensor("l1_a3T", [D, 256], F32, kind="ExternalInput")
    a2T = nc.dram_tensor("l1_a2T", [7168, 256], F32, kind="ExternalInput")
    b1T = nc.dram_tensor("l1_b1T", [256, 7168], F32, kind="ExternalInput")
    b3T = nc.dram_tensor("l1_b3T", [256, 7168], F32, kind="ExternalInput")
    b2T = nc.dram_tensor("l1_b2T", [256, D], F32, kind="ExternalInput")
    idb = nc.dram_tensor("ident_bf", [128, 128], BF16, kind="ExternalInput")
    idf = nc.dram_tensor("ident_f32", [128, 128], F32, kind="ExternalInput")
    g_fm = nc.dram_tensor("g_fm", [D, T], F32, kind="ExternalOutput")
    x2d = [nc.dram_tensor(f"x2_scr{k}", [7168, T], BF16) for k in range(2)]
    w2bf = nc.dram_tensor("w2bf_scr", [7168, D], BF16)
    rwd = nc.dram_tensor("rw_scr", [2, T], F32)

    FC = 56
    arena = p.sbuf("arena", [128, 61440], BF16)
    xn_sb = arena[:, 0:32768]
    ma_sb = arena[:, 32768:49152]
    x2r = [arena[:, 0:28672], arena[:, 32768:61440]]
    maskf = p.sbuf("maskf", [128, 4 * T], BF16)
    probs = p.sbuf("probs1", [128, 16 * 8], F32)
    m8t = p.sbuf("m8t", [128, 8], F32)
    rep1 = p.sbuf("rep1", [128, 8], F32)
    rep2 = p.sbuf("rep2", [128, 8], F32)
    scr1 = p.sbuf("scr1", [128, 8], F32)
    scr2 = p.sbuf("scr2", [128, 8], F32)
    mk1 = p.sbuf("mk1", [128, 8], F32)
    mk2 = p.sbuf("mk2", [128, 8], F32)
    m256 = p.sbuf("m256", [128, 256], BF16)
    srow = p.sbuf("srow1", [128, 8], F32)
    exp_sb = p.sbuf("exp1", [128, 8], F32)
    rw_tok = p.sbuf("rw_tok", [128, 32], F32)
    rwT_sb = p.sbuf("rwT_sb", [32, 128], F32)
    rwrt = [p.sbuf(f"rwrt{k}", [1, 512], F32) for k in range(2)]
    rwb2 = [p.sbuf(f"rwb2_{k}", [128, 512], F32) for k in range(2)]
    wg_sb = p.sbuf("wg_sb", [128, 16 * 8], BF16)
    idb_sb = p.sbuf("idb1_sb", [128, 128], BF16)
    idf_sb = p.sbuf("idf1_sb", [128, 128], F32)
    aT_sb = p.sbuf("aT_sb", [128, 16 * 256], BF16)
    wst = [p.sbuf(f"wst{i}", [128, 16 * 128], BF16) for i in range(4)]
    bst = [p.sbuf(f"bst{i}", [128, 512], BF16) for i in range(2)]
    w2st = [p.sbuf(f"w2st{i}", [128, 256], BF16) for i in range(2)]
    w2pc = p.sbuf("w2pc", [128, 2048], BF16)
    a2st = [p.sbuf(f"a2st{i}", [128, 256], BF16) for i in range(2)]
    b2_sb = p.sbuf("b2_sb", [128, 2 * 2048], BF16)
    b1t = p.sbuf("b1t", [128, 512], BF16)
    b3t = p.sbuf("b3t", [128, 512], BF16)
    x1s = p.sbuf("x1s", [128, 512], BF16)
    x2t = [p.sbuf(f"x2t{i}", [128, 512], BF16) for i in range(2)]
    ma2 = p.sbuf("ma2", [128, 4 * 512], BF16)
    out2 = [p.sbuf(f"out2_{i}", [128, 512], F32) for i in range(2)]
    tmp_sb = p.sbuf("tmp1_sb", [128, 512], F32)
    g_sb = [p.sbuf(f"g1_{i}", [128, 512], F32) for i in range(2)]
    ones_f32 = p.sbuf("ones1_f32", [1, 128], F32)

    ps8 = [p.psum(f"ps8_{i}", [128, 512]) for i in range(8)]

    s_xl = p.sem("s_xl"); s_cl = p.sem("s_cl"); s_gl = p.sem("s_gl")
    s_rt = p.sem("s_rt"); s_sm = p.sem("s_sm"); s_sx = p.sem("s_sx")
    s_pr = p.sem("s_pr"); s_m256 = p.sem("s_m256"); s_tp = p.sem("s_tp")
    s_mf = p.sem("s_mf"); s_rwt = p.sem("s_rwt"); s_rwte = p.sem("s_rwte")
    s_rwd = p.sem("s_rwd")
    s_am = p.sem("s_am"); s_ae = p.sem("s_ae"); s_a3l = p.sem("s_a3l")
    s_ws = [p.sem(f"s_ws{i}") for i in range(2)]
    s_bs = [p.sem(f"s_bs{i}") for i in range(2)]
    s_bm = p.sem("s_bm"); s_be = p.sem("s_be"); s_x1 = p.sem("s_x1")
    s_sil = p.sem("s_sil"); s_x2 = p.sem("s_x2")
    s_x2st = [p.sem(f"s_x2st{i}") for i in range(2)]
    s_w2p = [p.sem(f"s_w2p{i}") for i in range(2)]
    s_w2w = [p.sem(f"s_w2w{i}") for i in range(2)]
    s_x2l = [p.sem(f"s_x2l{i}") for i in range(2)]
    s_a2l = [p.sem(f"s_a2l{i}") for i in range(2)]
    s_a2mm = p.sem("s_a2mm"); s_a2m = p.sem("s_a2m"); s_a2e = p.sem("s_a2e")
    s_w2l = [p.sem(f"s_w2l{i}") for i in range(2)]
    s_w2c = p.sem("s_w2c"); s_dm = p.sem("s_dm"); s_oa = p.sem("s_oa")
    s_rb = p.sem("s_rb"); s_rbe = p.sem("s_rbe"); s_g = p.sem("s_g")
    s_gst = [p.sem(f"s_gst1{i}") for i in range(2)]
    s_cst = p.sem("s_cst1")

    p.start()

    @p.block.sync
    def _(sync):
        sync.dma_start(out=xn_sb, in_=xnl.ap().rearrange("(dc pp) t -> pp dc t", pp=128)).then_inc(s_xl, 16)
        sync.dma_start(out=idb_sb[:], in_=idb.ap()).then_inc(s_cl, 16)
        sync.dma_start(out=idf_sb[:], in_=idf.ap()).then_inc(s_cl, 16)
        sync.wait_ge(s_rwte, 1)
        for k in range(2):
            sync.dma_start(out=rwd.ap()[k:k + 1, :].rearrange("o (i pp) -> (o i) pp", i=16),
                           in_=rwT_sb[k * 16:(k + 1) * 16, :]).then_inc(s_rwd, 16)
        sync.wait_ge(s_rwd, 32)
        for fc in range(FC):
            for tch in range(4):
                for k in range(2):
                    ft = fc * 4 + tch
                    sync.wait_ge(s_x2, 2 * ft + k + 1)
                    sync.dma_start(out=x2d[k].ap()[fc * 128:(fc + 1) * 128, tch * 512:(tch + 1) * 512],
                                   in_=x2t[k][:]).then_inc(s_x2st[k], 16)
        for tch in range(4):
            sync.wait_ge(s_rwd, 32)
            if tch >= 1:
                sync.wait_ge(s_oa, 16 * tch)
            sync.dma_start(out=rwrt[0][:], in_=rwd.ap()[0:1, tch * 512:(tch + 1) * 512]).then_inc(s_rwd, 16)
            sync.dma_start(out=rwrt[1][:], in_=rwd.ap()[1:2, tch * 512:(tch + 1) * 512]).then_inc(s_rwd, 16)
            for k in range(2):
                sync.wait_ge(s_x2st[k], 16 * 224)
                if tch == 0:
                    sync.wait_ge(s_x1, 896)
                else:
                    sync.wait_ge(s_dm, 8 * tch)
                    sync.wait_ge(s_a2mm, 56 * tch)
                sync.dma_start(out=x2r[k],
                               in_=x2d[k].ap()[:, tch * 512:(tch + 1) * 512]
                               .rearrange("(fc pp) t -> pp fc t", pp=128)).then_inc(s_x2l[k], 16)
            for gi in range(16):
                ti = tch * 16 + gi
                sync.wait_ge(s_g, ti + 1)
                sync.dma_start(out=g_fm.ap()[gi * 128:(gi + 1) * 128, tch * 512:(tch + 1) * 512],
                               in_=g_sb[ti % 2][:]).then_inc(s_gst[ti % 2], 16)
        sync.wait_ge(s_gst[0], 16 * 32)
        sync.wait_ge(s_gst[1], 16 * 32)

    @p.block.gpsimd
    def _(g):
        g.dma_start(out=wg_sb[:], in_=wgT.ap().rearrange("(dc pp) e -> pp dc e", pp=128)).then_inc(s_gl, 16)
        g.dma_start(out=aT_sb[:], in_=a1T.ap().rearrange("(dc pp) e -> pp dc e", pp=128)).then_inc(s_gl, 16)
        g.dma_start(out=b2_sb[:], in_=b2T.ap().rearrange("(erc pp) e -> pp erc e", pp=128)).then_inc(s_gl, 16)
        g.wait_ge(s_am, 8)
        g.dma_start(out=aT_sb[:], in_=a3T.ap().rearrange("(dc pp) e -> pp dc e", pp=128)).then_inc(s_a3l, 16)
        for fc in range(FC):
            sl = fc % 2
            if fc >= 2:
                g.wait_ge(s_bm, 8 * (fc - 1))
                g.wait_ge(s_x1, 16 * (fc - 1))
            g.dma_start(out=wst[sl][:], in_=w1T.ap()[:, fc * 128:(fc + 1) * 128]
                        .rearrange("(dc pp) f -> pp dc f", pp=128)).then_inc(s_ws[sl], 16)
            g.dma_start(out=wst[2 + sl][:], in_=w3T.ap()[:, fc * 128:(fc + 1) * 128]
                        .rearrange("(dc pp) f -> pp dc f", pp=128)).then_inc(s_ws[sl], 16)
            g.dma_start(out=bst[sl][:, 0:256], in_=b1T.ap()[:, fc * 128:(fc + 1) * 128]
                        .rearrange("(erc pp) f -> pp erc f", pp=128)).then_inc(s_bs[sl], 16)
            g.dma_start(out=bst[sl][:, 256:512], in_=b3T.ap()[:, fc * 128:(fc + 1) * 128]
                        .rearrange("(erc pp) f -> pp erc f", pp=128)).then_inc(s_bs[sl], 16)
        for fc in range(FC):
            if fc >= 1:
                g.wait_ge(s_w2w[0], 16 * fc)
            g.dma_start(out=w2pc[:], in_=w2T.ap()[fc * 128:(fc + 1) * 128, :]).then_inc(s_w2p[0], 16)
            g.wait_ge(s_w2p[0], 16 * (fc + 1))
            g.dma_start(out=w2bf.ap()[fc * 128:(fc + 1) * 128, :], in_=w2pc[:]).then_inc(s_w2w[0], 16)
        for tch in range(4):
            for fc in range(FC):
                ai = tch * FC + fc
                sl = ai % 2
                if ai >= 2:
                    g.wait_ge(s_a2mm, ai - 1)
                g.dma_start(out=a2st[sl][:], in_=a2T.ap()[fc * 128:(fc + 1) * 128, :]).then_inc(s_a2l[sl], 16)
            for ot in range(8):
                for fc in range(FC):
                    wi = (tch * 8 + ot) * FC + fc
                    sl = wi % 2
                    if wi == 0:
                        g.wait_ge(s_w2w[0], 16 * FC)
                    if wi >= 2:
                        g.wait_ge(s_w2c, wi - 1)
                    g.dma_start(out=w2st[sl][:],
                                in_=bass.AP(w2bf, fc * 128 * 2048 + ot * 256, [[2048, 128], [1, 256]])
                                ).then_inc(s_w2l[sl], 16)

    @p.block.tensor
    def _(t):
        t.wait_ge(s_xl, 16)
        t.wait_ge(s_gl, 48)
        t.wait_ge(s_cst, 1)
        t.wait_ge(s_cl, 32)
        pbf = ps8[7].bitcast(BF16)
        ntp = 0
        def emit_tp_pair(tt, ik):
            nonlocal ntp
            tt.wait_ge(s_m256, ik + 1)
            for erc in range(2):
                if ntp >= 1:
                    tt.wait_ge(s_mf, ntp)
                tt.transpose(pbf[:, 0:128], m256[:, erc * 128:(erc + 1) * 128],
                             idb_sb[:]).then_inc(s_tp, 1)
                ntp += 1
        for i in range(16):
            if i >= 1:
                t.wait_ge(s_sx, i)
            for dc in range(16):
                mi = t.matmul(ps8[6][:, 0:8], xn_sb[:, dc * T + i * 128: dc * T + i * 128 + 128],
                              wg_sb[:, dc * 8:(dc + 1) * 8], start=(dc == 0), stop=(dc == 15))
                if dc == 15:
                    mi.then_inc(s_rt, 1)
            if i >= 1:
                emit_tp_pair(t, 2 * (i - 1))
                emit_tp_pair(t, 2 * (i - 1) + 1)
        emit_tp_pair(t, 30)
        emit_tp_pair(t, 31)
        t.wait_ge(s_pr, 16)
        t.wait_ge(s_mf, ntp)
        t.transpose(ps8[7][0:32, 0:128], rw_tok[:], idf_sb[:]).then_inc(s_rwt, 1)
        for w in range(2):
            if w == 1:
                t.wait_ge(s_a3l, 16)
            for erc in range(2):
                for tch in range(4):
                    gidx = (w * 2 + erc) * 4 + tch
                    if gidx >= 2:
                        t.wait_ge(s_ae, gidx - 1)
                    for dc in range(16):
                        mi = t.matmul(ps8[gidx % 2][:],
                                      aT_sb[:, dc * 256 + erc * 128: dc * 256 + (erc + 1) * 128],
                                      xn_sb[:, dc * T + tch * 512: dc * T + (tch + 1) * 512],
                                      start=(dc == 0), stop=(dc == 15))
                        if dc == 15:
                            mi.then_inc(s_am, 1)
        for fc in range(FC):
            sl = fc % 2
            t.wait_ge(s_ws[sl], 32 * (fc // 2 + 1))
            t.wait_ge(s_bs[sl], 32 * (fc // 2 + 1))
            for tch in range(4):
                ft = fc * 4 + tch
                if ft >= 1:
                    t.wait_ge(s_be, 2 * ft - 1)
                for dc in range(16):
                    mi = t.matmul(ps8[2][:], wst[sl][:, dc * 128:(dc + 1) * 128],
                                  xn_sb[:, dc * T + tch * 512: dc * T + (tch + 1) * 512],
                                  start=(dc == 0), stop=(dc == 15))
                    if dc == 15:
                        mi.then_inc(s_bm, 1)
                if ft >= 1:
                    t.wait_ge(s_be, 2 * ft)
                for dc in range(16):
                    mi = t.matmul(ps8[3][:], wst[2 + sl][:, dc * 128:(dc + 1) * 128],
                                  xn_sb[:, dc * T + tch * 512: dc * T + (tch + 1) * 512],
                                  start=(dc == 0), stop=(dc == 15))
                    if dc == 15:
                        mi.then_inc(s_bm, 1)
                for k in range(2):
                    t.wait_ge(s_be, 2 * ft + 1)
                    if 2 * ft + k >= 1:
                        t.wait_ge(s_sil, 2 * ft + k)
                    for erc in range(2):
                        t.matmul(ps8[4][:], bst[sl][:, 0 + erc * 128: 0 + (erc + 1) * 128],
                                 ma_sb[:, ((0 * 2 + k) * 2 + erc) * 2048 + tch * 512:
                                       ((0 * 2 + k) * 2 + erc) * 2048 + (tch + 1) * 512],
                                 start=(erc == 0), stop=False)
                    t.matmul(ps8[4][:], idb_sb[:], b1t[:], start=False, stop=True).then_inc(s_x1, 1)
                    t.wait_ge(s_be, 2 * ft + 2)
                    if 2 * ft + k >= 1:
                        t.wait_ge(s_x2, 2 * ft + k)
                    for erc in range(2):
                        t.matmul(ps8[5][:], bst[sl][:, 256 + erc * 128: 256 + (erc + 1) * 128],
                                 ma_sb[:, ((1 * 2 + k) * 2 + erc) * 2048 + tch * 512:
                                       ((1 * 2 + k) * 2 + erc) * 2048 + (tch + 1) * 512],
                                 start=(erc == 0), stop=False)
                    t.matmul(ps8[5][:], idb_sb[:], b3t[:], start=False, stop=True).then_inc(s_x1, 1)
        for tch in range(4):
            t.wait_ge(s_oa, 16 * tch)
            t.wait_ge(s_rwd, 32 + 32 * (tch + 1))
            for k in range(2):
                t.matmul(ps8[4 + k][:], ones_f32[:], rwrt[k][:],
                         start=True, stop=True).then_inc(s_rb, 1)
            for k in range(2):
                t.wait_ge(s_x2l[k], 16 * (tch + 1))
            t.wait_ge(s_a2e, 4 * tch)
            for fc in range(FC):
                ai = tch * FC + fc
                sl = ai % 2
                t.wait_ge(s_a2l[sl], 16 * (ai // 2 + 1))
                for erc in range(2):
                    for k in range(2):
                        gq = erc * 2 + k
                        mi = t.matmul(ps8[gq][:], a2st[sl][:, erc * 128:(erc + 1) * 128],
                                      x2r[k][:, fc * 512:(fc + 1) * 512],
                                      start=(fc == 0), stop=(fc == FC - 1))
                        if erc == 1 and k == 1:
                            mi.then_inc(s_a2mm, 1)
                        if fc == FC - 1 and erc == 1 and k == 1:
                            mi.then_inc(s_a2m, 1) if False else None
            for ot in range(8):
                t.wait_ge(s_oa, 16 * tch + 2 * ot)
                t.wait_ge(s_rbe, 2 * (tch + 1))
                if tch == 0 and ot == 0:
                    t.wait_ge(s_a2e, 4)
                for fc in range(FC):
                    wi = (tch * 8 + ot) * FC + fc
                    sl = wi % 2
                    t.wait_ge(s_w2l[sl], 16 * (wi // 2 + 1))
                    for hcl in range(2):
                        for k in range(2):
                            gq = 4 + hcl * 2 + k
                            mi = t.matmul(ps8[gq][:], w2st[sl][:, hcl * 128:(hcl + 1) * 128],
                                          x2r[k][:, fc * 512:(fc + 1) * 512],
                                          start=(fc == 0), stop=False)
                            if hcl == 1 and k == 1:
                                mi.then_inc(s_w2c, 1)
                t.wait_ge(s_a2e, 4 * (tch + 1))
                hc0 = ot * 2
                for hcl in range(2):
                    for k in range(2):
                        gq = 4 + hcl * 2 + k
                        for erc in range(2):
                            mi = t.matmul(ps8[gq][:],
                                          b2_sb[:, erc * 2048 + (hc0 + hcl) * 128: erc * 2048 + (hc0 + hcl) * 128 + 128],
                                          ma2[:, (k * 2 + erc) * 512:(k * 2 + erc + 1) * 512],
                                          start=False, stop=(erc == 1))
                        if hcl == 1 and k == 1:
                            mi.then_inc(s_dm, 1)

    @p.block.scalar
    def _(sc):
        for i in range(16):
            sc.wait_ge(s_rt, i + 1)
            sc.wait_ge(s_sm, i + 1)
            sc.activation(exp_sb[:], ps8[6][:, 0:8], AF.Exp, bias=srow[:, 1:2], scale=1.0,
                          accum_out=srow[:, 2:3]).then_inc(s_sx, 1)
        for fc in range(FC):
            for tch in range(4):
                ft = fc * 4 + tch
                sc.wait_ge(s_bm, 2 * ft + 1)
                sc.copy(b1t[:], ps8[2][:]).then_inc(s_be, 1)
                sc.wait_ge(s_bm, 2 * ft + 2)
                sc.copy(b3t[:], ps8[3][:]).then_inc(s_be, 1)
                for k in range(2):
                    sc.wait_ge(s_x1, 4 * ft + 2 * k + 1)
                    if 2 * ft + k >= 1:
                        sc.wait_ge(s_x2, 2 * ft + k)
                    sc.activation(x1s[:], ps8[4][:], AF.Silu).then_inc(s_sil, 1)
        for tch in range(4):
            for k in range(2):
                sc.wait_ge(s_rb, 2 * tch + k + 1)
                sc.copy(rwb2[k][:], ps8[4 + k][:]).then_inc(s_rbe, 1)
            for gi in range(16):
                ti = tch * 16 + gi
                sc.wait_ge(s_oa, ti + 1)
                if ti >= 2:
                    sc.wait_ge(s_gst[ti % 2], 16 * (ti // 2))
                sc.activation(g_sb[ti % 2][:], out2[ti % 2][:], AF.Silu).then_inc(s_g, 1)

    @p.block.vector
    def _(v):
        v.memset(rep1[:], 2.0)
        v.memset(rep2[:], 2.0)
        v.memset(ones_f32[:], 1.0).then_inc(s_cst, 1)
        for i in range(16):
            v.wait_ge(s_rt, i + 1)
            p.guard(v, v.reduce_max(srow[:, 0:1], ps8[6][:, 0:8], axis=AX.X))
            v.tensor_scalar_mul(srow[:, 1:2], srow[:, 0:1], -1.0).then_inc(s_sm, 1)
            v.wait_ge(s_sx, i + 1)
            p.guard(v, v.reciprocal(srow[:, 3:4], srow[:, 2:3]))
            p.guard(v, v.tensor_scalar_mul(probs[:, i * 8:(i + 1) * 8], exp_sb[:], srow[:, 3:4]))
            p.guard(v, v.max(out=m8t[:], in_=probs[:, i * 8:(i + 1) * 8]))
            p.guard(v, v.tensor_copy(rep1[:, 0:1], m8t[:, 0:1]))
            p.guard(v, v.tensor_copy(rep2[:, 0:1], m8t[:, 1:2]))
            p.guard(v, v.match_replace(out=scr1[:], in_to_replace=rep1[:],
                                       in_values=probs[:, i * 8:(i + 1) * 8], imm_value=-1.0))
            p.guard(v, v.tensor_tensor(mk1[:], scr1[:], probs[:, i * 8:(i + 1) * 8], ALU.not_equal))
            p.guard(v, v.match_replace(out=scr2[:], in_to_replace=rep2[:],
                                       in_values=scr1[:], imm_value=-1.0))
            p.guard(v, v.tensor_tensor(mk2[:], scr2[:], scr1[:], ALU.not_equal))
            p.guard(v, v.tensor_add(srow[:, 4:5], m8t[:, 0:1], m8t[:, 1:2]))
            p.guard(v, v.reciprocal(srow[:, 5:6], srow[:, 4:5]))
            p.guard(v, v.tensor_tensor(rw_tok[:, i:i + 1], m8t[:, 0:1], srow[:, 5:6], ALU.mult))
            gi2 = v.tensor_tensor(rw_tok[:, 16 + i:16 + i + 1], m8t[:, 1:2], srow[:, 5:6], ALU.mult)
            gi2.then_inc(s_pr, 1)
            p.guard(v, gi2) if False else None
            for k in range(2):
                ik = i * 2 + k
                mkk = mk1 if k == 0 else mk2
                if ik >= 1:
                    pi, pk = (ik - 1) // 2, (ik - 1) % 2
                    for erc in range(2):
                        v.wait_ge(s_tp, 2 * (ik - 1) + erc + 1)
                        v.tensor_copy(maskf[:, (pk * 2 + erc) * T + pi * 128: (pk * 2 + erc) * T + pi * 128 + 128],
                                      ps8[7].bitcast(BF16)[:, 0:128]).then_inc(s_mf, 1)
                src_ap = bass.AP(mkk.ap().tensor, 0, [[8, 128], [1, 8], [0, 32]])
                v.tensor_copy(m256.ap().rearrange("pp (e r) -> pp e r", e=8), src_ap).then_inc(s_m256, 1)
        for erc in range(2):
            v.wait_ge(s_tp, 62 + erc + 1)
            v.tensor_copy(maskf[:, (1 * 2 + erc) * T + 15 * 128: (1 * 2 + erc) * T + 15 * 128 + 128],
                          ps8[7].bitcast(BF16)[:, 0:128]).then_inc(s_mf, 1)
        v.wait_ge(s_rwt, 1)
        v.tensor_copy(rwT_sb[:], ps8[7][0:32, 0:128]).then_inc(s_rwte, 1)
        v.wait_ge(s_mf, 64)
        for w in range(2):
            for erc in range(2):
                for tch in range(4):
                    gidx = (w * 2 + erc) * 4 + tch
                    v.wait_ge(s_am, gidx + 1)
                    for k in range(2):
                        ii = v.tensor_tensor(
                            ma_sb[:, ((w * 2 + k) * 2 + erc) * 2048 + tch * 512:
                                  ((w * 2 + k) * 2 + erc) * 2048 + (tch + 1) * 512],
                            ps8[gidx % 2][:],
                            maskf[:, (k * 2 + erc) * T + tch * 512: (k * 2 + erc) * T + (tch + 1) * 512],
                            ALU.mult)
                        if k == 1:
                            ii.then_inc(s_ae, 1)
        for fc in range(FC):
            for tch in range(4):
                ft = fc * 4 + tch
                for k in range(2):
                    v.wait_ge(s_sil, 2 * ft + k + 1)
                    v.wait_ge(s_x1, 4 * ft + 2 * k + 2)
                    v.wait_ge(s_x2st[k], 16 * ft)
                    v.tensor_tensor(x2t[k][:], x1s[:], ps8[5][:], ALU.mult).then_inc(s_x2, 1)
        for tch in range(4):
            v.wait_ge(s_a2mm, tch * FC + FC)
            for gq in range(4):
                erc, k = gq // 2, gq % 2
                ii = v.tensor_tensor(ma2[:, (k * 2 + erc) * 512:(k * 2 + erc + 1) * 512],
                                     ps8[gq][:],
                                     maskf[:, (k * 2 + erc) * T + tch * 512: (k * 2 + erc) * T + (tch + 1) * 512],
                                     ALU.mult)
                ii.then_inc(s_a2e, 1)
            for ot in range(8):
                v.wait_ge(s_dm, tch * 8 + ot + 1)
                v.wait_ge(s_rbe, 2 * (tch + 1))
                for hcl in range(2):
                    ti = tch * 16 + ot * 2 + hcl
                    if ti >= 2:
                        v.wait_ge(s_g, ti - 1)
                    p.guard(v, v.tensor_tensor(out2[ti % 2][:], ps8[4 + hcl * 2][:], rwb2[0][:], ALU.mult))
                    p.guard(v, v.tensor_tensor(tmp_sb[:], ps8[4 + hcl * 2 + 1][:], rwb2[1][:], ALU.mult))
                    v.tensor_tensor(out2[ti % 2][:], out2[ti % 2][:], tmp_sb[:], ALU.add).then_inc(s_oa, 1)

    p.close()
    return p.nc


def kernel(**inputs):
    maps = stage_inputs(inputs)
    CORES = list(range(8))

    def run(buildfn, outs, *args, remap=None):
        nc = buildfn(*args)
        res = run_bass_kernel_spmd(nc, filter_map(maps, nc), core_ids=CORES)
        for c in CORES:
            for k in outs:
                maps[c][(remap or {}).get(k, k)] = np.asarray(res.results[c][k])
        return res

    run(build_ln, ["xn_full", "xn_loc"])
    run(build_qkv, ["q_fm", "k_fm", "v_blk"], 0)
    run(build_attn, ["av_full"], 0)
    run(build_ffn0, ["g_fm"])
    run(build_ep, ["x_out"], 0, remap={"x_out": "x_in"})
    run(build_ln, ["xn_full", "xn_loc"])
    run(build_qkv, ["q_fm", "k_fm", "v_blk"], 1)
    run(build_rope, ["q_ro", "k_ro"])
    run(build_attn, ["av_full"], 1)
    run(build_ffn1, ["g_fm"])
    res = run(build_ep, ["x_out"], 1)

    out = np.zeros((2, 8192, 2048), np.float32)
    for c in CORES:
        b, j = c // 4, c % 4
        got = np.asarray(res.results[c]["x_out"], np.float32)
        out[b, j * 1024:(j + 1) * 1024] = got[0:1024]
        out[b, (7 - j) * 1024:(8 - j) * 1024] = got[1024:2048]
    return out


# version 02 (final)
# speedup vs baseline: 1.0032x; 1.0032x over previous
import numpy as np
import ml_dtypes
import concourse.bass as bass
import concourse.mybir as mybir
from concourse.bass_utils import run_bass_kernel_spmd
from contextlib import ExitStack

F32 = mybir.dt.float32
BF16 = mybir.dt.bfloat16
I32 = mybir.dt.int32
AF = mybir.ActivationFunctionType
ALU = mybir.AluOpType
AX = mybir.AxisListType
BF = ml_dtypes.bfloat16

D = 2048
T = 2048
SEQ = 8192
G4 = [[0, 1, 2, 3], [4, 5, 6, 7]]
G8 = [[0, 1, 2, 3, 4, 5, 6, 7]]
EXP_C = 16.0
SCALE = float(1.0 / np.sqrt(128.0))


def span_loc(g0):
    p = g0 // 1024
    j = min(p, 7 - p)
    col0 = 0 if p <= 3 else 1024
    return j, col0 + (g0 % 1024)


class P:
    def __init__(self):
        self.nc = bass.Bass("TRN2", target_bir_lowering=False)
        self.ctx = ExitStack()
        self.block = None
        self._gsem = {}
    def guard(self, eng, instr):
        key = id(eng)
        if key not in self._gsem:
            self._gsem[key] = [self.ctx.enter_context(self.nc.semaphore(f"g{len(self._gsem)}")), 0]
        g = self._gsem[key]
        g[1] += 1
        instr.then_inc(g[0], 1)
        eng.wait_ge(g[0], g[1])
        return instr
    def start(self):
        self.block = self.ctx.enter_context(self.nc.Block())
        return self.block
    def sem(self, name):
        return self.ctx.enter_context(self.nc.semaphore(name))
    def sbuf(self, name, shape, dtype):
        return self.ctx.enter_context(self.nc.sbuf_tensor(name, shape, dtype))
    def psum(self, name, shape, dtype=F32):
        return self.ctx.enter_context(self.nc.psum_tensor(name, shape, dtype))
    def close(self):
        self.ctx.close()


def build_ln(debug=False):
    p = P()
    nc = p.nc
    xin = nc.dram_tensor("x_in", [T, D], F32, kind="ExternalInput")
    ident = nc.dram_tensor("ident_bf", [128, 128], BF16, kind="ExternalInput")
    xn_full = nc.dram_tensor("xn_full", [4 * D, T], BF16, kind="ExternalOutput")
    xn_loc = nc.dram_tensor("xn_loc", [D, T], BF16, kind="ExternalOutput")
    xtd = nc.dram_tensor("xn_tok_dbg", [128, 16 * D], BF16, kind="ExternalOutput") if debug else None
    cc_in = nc.dram_tensor("cc_xn_in", [D, T], BF16)
    cc_out = nc.dram_tensor("cc_xn_out", [4 * D, T], BF16)

    xt = [p.sbuf(f"xt{k}", [128, D], F32) for k in range(2)]
    xn_tok = p.sbuf("xn_tok", [128, 16 * D], BF16)
    sq = p.sbuf("sq", [128, D], F32)
    msq = p.sbuf("msq", [128, 1], F32)
    sums = p.sbuf("sums", [128, 16], F32)
    mean = p.sbuf("mean", [128, 16], F32)
    sqa = p.sbuf("sqa", [128, 16], F32)
    var = p.sbuf("var", [128, 16], F32)
    std = p.sbuf("std", [128, 16], F32)
    rstd = p.sbuf("rstd", [128, 16], F32)
    eps = p.sbuf("eps", [128, 1], F32)
    idb = p.sbuf("idb", [128, 128], BF16)
    fmst = [p.sbuf(f"fmst{k}", [128, T], BF16) for k in range(2)]
    pst = [p.psum(f"pst{k}", [128, 128], BF16) for k in range(2)]

    s_ld = [p.sem("s_ld0"), p.sem("s_ld1")]; s_ldc = p.sem("s_ldc")
    s_sq = p.sem("s_sq"); s_var = p.sem("s_var")
    s_std = p.sem("s_std"); s_xn = p.sem("s_xn"); s_tp = p.sem("s_tp")
    s_ev = p.sem("s_ev"); s_fm = [p.sem("s_fm0"), p.sem("s_fm1")]; s_cc = p.sem("s_cc")
    s_done = p.sem("s_done")
    p.start()

    @p.block.sync
    def _(sync):
        sync.dma_start(out=idb[:], in_=ident.ap()).then_inc(s_ldc, 16)
        for i in range(16):
            if i >= 2:
                sync.wait_ge(s_sq, i - 1)
                sync.wait_ge(s_xn, i - 1)
            sync.dma_start(out=xt[i % 2][:], in_=xin.ap()[i * 128:(i + 1) * 128, :]).then_inc(s_ld[i % 2], 16)
        for dc in range(16):
            sync.wait_ge(s_ev, 16 * (dc + 1))
            sync.dma_start(out=cc_in.ap()[dc * 128:(dc + 1) * 128, :], in_=fmst[dc % 2][:]).then_inc(s_fm[dc % 2], 16)
            sync.dma_start(out=xn_loc.ap()[dc * 128:(dc + 1) * 128, :], in_=fmst[dc % 2][:]).then_inc(s_fm[dc % 2], 16)
        if debug:
            sync.dma_start(out=xtd.ap(), in_=xn_tok[:]).then_inc(s_done, 16)
        sync.wait_ge(s_cc, 1)
        sync.dma_start(out=xn_full.ap(), in_=cc_out.ap()).then_inc(s_done, 16)
        sync.wait_ge(s_done, 16 * (2 if debug else 1))

    @p.block.scalar
    def _(sc):
        for i in range(16):
            sc.wait_ge(s_ld[i % 2], 16 * (i // 2 + 1))
            sc.activation(sq[:], xt[i % 2][:], AF.Square,
                          accum_out=sqa[:, i:i + 1]).then_inc(s_sq, 1)
            sc.wait_ge(s_var, i + 1)
            sc.activation(std[:, i:i + 1], var[:, i:i + 1], AF.Sqrt,
                          bias=eps[:], scale=1.0).then_inc(s_std, 1)

    @p.block.vector
    def _(v):
        v.memset(eps[:], 1e-5)
        for i in range(16):
            v.wait_ge(s_ld[i % 2], 16 * (i // 2 + 1))
            p.guard(v, v.reduce_sum(sums[:, i:i + 1], xt[i % 2][:], axis=AX.X))
            p.guard(v, v.tensor_scalar_mul(mean[:, i:i + 1], sums[:, i:i + 1], 1.0 / D))
            v.wait_ge(s_sq, i + 1)
            p.guard(v, v.tensor_scalar_mul(var[:, i:i + 1], sqa[:, i:i + 1], 1.0 / D))
            p.guard(v, v.tensor_tensor(msq[:], mean[:, i:i + 1], mean[:, i:i + 1], ALU.mult))
            v.tensor_tensor(var[:, i:i + 1], var[:, i:i + 1], msq[:],
                            ALU.subtract).then_inc(s_var, 1)
            v.wait_ge(s_std, i + 1)
            p.guard(v, v.reciprocal(rstd[:, i:i + 1], std[:, i:i + 1]))
            v.tensor_scalar(xn_tok[:, i * D:(i + 1) * D], xt[i % 2][:],
                            mean[:, i:i + 1], rstd[:, i:i + 1],
                            ALU.subtract, ALU.mult).then_inc(s_xn, 1)
        for dc in range(16):
            if dc >= 2:
                v.wait_ge(s_fm[dc % 2], 32 * (dc // 2))
            for i in range(16):
                idx = dc * 16 + i
                v.wait_ge(s_tp, idx + 1)
                v.tensor_copy(fmst[dc % 2][:, i * 128:(i + 1) * 128],
                              pst[idx % 2][:]).then_inc(s_ev, 1)

    @p.block.tensor
    def _(t):
        t.wait_ge(s_ldc, 16)
        t.wait_ge(s_xn, 16)
        for dc in range(16):
            for i in range(16):
                idx = dc * 16 + i
                if idx >= 2:
                    t.wait_ge(s_ev, idx - 1)
                t.transpose(pst[idx % 2][:], xn_tok[:, i * D + dc * 128: i * D + dc * 128 + 128],
                            idb[:]).then_inc(s_tp, 1)

    @p.block.gpsimd
    def _(g):
        g.wait_ge(s_fm[0], 32 * 8)
        g.wait_ge(s_fm[1], 32 * 8)
        g.collective_compute("AllGather", ALU.bypass, replica_groups=G4,
                             ins=[cc_in.ap().opt()], outs=[cc_out.ap().opt()]).then_inc(s_cc)

    p.close()
    return p.nc


def stage_inputs(inputs):
    f = np.float32
    x = np.asarray(inputs["L0_attn_x"], f)
    maps = []
    ident_bf = np.eye(128, dtype=BF)
    dm = np.zeros((4, 128, 512), f)
    for dj in range(4):
        m = np.zeros((128, 512), f)
        m[:, dj * 128:(dj + 1) * 128] = np.triu(np.ones((128, 128), f))
        m[:, (dj + 1) * 128:] = 1.0
        dm[dj] = m
    diag_masks = np.ascontiguousarray(dm.reshape(512, 512)).astype(BF)
    ki_ = np.arange(128)[:, None]
    qi_ = np.arange(512)[None, :]
    half = 64
    theta = 1.0 / (10000.0 ** (np.arange(half, dtype=np.float64) / half))
    mm = np.arange(SEQ, dtype=np.float64)
    freqs = np.outer(mm, theta)
    cos_fm = np.ascontiguousarray(np.cos(freqs).T.astype(f)).astype(BF)
    sin_fm = np.ascontiguousarray(np.sin(freqs).T.astype(f)).astype(BF)
    ki = np.arange(128, dtype=f)

    for c in range(8):
        b, j = c // 4, c % 4
        H4 = slice(4 * j * 128, (4 * j + 4) * 128)
        d = {}
        d["x_in"] = np.ascontiguousarray(
            np.concatenate([x[b, j * 1024:(j + 1) * 1024],
                            x[b, (7 - j) * 1024:(8 - j) * 1024]], axis=0))
        qw = np.asarray(inputs["L0_attn_q_w"], f)
        kw = np.asarray(inputs["L0_attn_k_w"], f)
        vw = np.asarray(inputs["L0_attn_v_w"], f)
        d["l0_qkvT"] = np.ascontiguousarray(
            np.concatenate([qw[H4].T, kw[H4].T, vw[H4].T], axis=1))
        d["l0_owT"] = np.ascontiguousarray(np.asarray(inputs["L0_attn_o_w"], f).T)
        d["l0_gate_wT"] = np.ascontiguousarray(np.asarray(inputs["L0_attn_gate_w"], f).T)
        d["l0_gate_b"] = np.ascontiguousarray(np.asarray(inputs["L0_attn_gate_b"], f).reshape(128, 1))
        d["l0_router_w"] = np.ascontiguousarray(np.asarray(inputs["L0_ffn_router_w"], f))
        d["l0_router_b"] = np.ascontiguousarray(np.asarray(inputs["L0_ffn_router_b"], f).reshape(1, 16))
        d["l0_expert_w"] = np.ascontiguousarray(np.asarray(inputs["L0_ffn_expert_w"], f))
        d["l0_expert_b"] = np.ascontiguousarray(
            np.asarray(inputs["L0_ffn_expert_b"], f).reshape(16, 128).T)
        d["l1_qkvT"] = np.ascontiguousarray(np.concatenate(
            [np.asarray(inputs["L1_attn_Wq"], f)[:, H4],
             np.asarray(inputs["L1_attn_Wk"], f)[:, j * 128:(j + 1) * 128],
             np.asarray(inputs["L1_attn_Wv"], f)[:, j * 128:(j + 1) * 128]], axis=1))
        d["l1_wo"] = np.ascontiguousarray(np.asarray(inputs["L1_attn_Wo"], f))
        d["l1_wgT"] = np.ascontiguousarray(np.asarray(inputs["L1_ffn_Wg"], f).T)
        d["l1_w1T"] = np.ascontiguousarray(np.asarray(inputs["L1_ffn_W1"], f).T)
        d["l1_w3T"] = np.ascontiguousarray(np.asarray(inputs["L1_ffn_W3"], f).T)
        d["l1_w2T"] = np.ascontiguousarray(np.asarray(inputs["L1_ffn_W2"], f).T)
        A1 = np.asarray(inputs["L1_ffn_A1"], f); A3 = np.asarray(inputs["L1_ffn_A3"], f)
        A2 = np.asarray(inputs["L1_ffn_A2"], f)
        B1 = np.asarray(inputs["L1_ffn_B1"], f); B3 = np.asarray(inputs["L1_ffn_B3"], f)
        B2 = np.asarray(inputs["L1_ffn_B2"], f)
        d["l1_a1T"] = np.ascontiguousarray(A1.transpose(2, 0, 1).reshape(2048, 256))
        d["l1_a3T"] = np.ascontiguousarray(A3.transpose(2, 0, 1).reshape(2048, 256))
        d["l1_a2T"] = np.ascontiguousarray(A2.transpose(2, 0, 1).reshape(7168, 256))
        d["l1_b1T"] = np.ascontiguousarray(B1.transpose(0, 2, 1).reshape(256, 7168))
        d["l1_b3T"] = np.ascontiguousarray(B3.transpose(0, 2, 1).reshape(256, 7168))
        d["l1_b2T"] = np.ascontiguousarray(B2.transpose(0, 2, 1).reshape(256, 2048))
        d["ident_bf"] = ident_bf
        d["ident_f32"] = np.eye(128, dtype=f)
        d["diag_masks"] = diag_masks
        d["cos_fm"] = cos_fm
        d["sin_fm"] = sin_fm
        hglob = np.arange(4 * j, 4 * j + 4, dtype=f)
        slopes = (2.0 ** (-8.0 * (hglob + 1.0) / 16.0)).astype(np.float64)
        scl = float(1.0 / np.sqrt(128.0))
        kvals = np.arange(SEQ, dtype=np.float64)
        def hilo(v):
            hi = v.astype(BF).astype(np.float64)
            lo = (v - hi).astype(BF)
            return hi.astype(BF), lo
        al_l = np.zeros((4, 4 * 64 * 128), BF)
        al_r = np.zeros((4, 4 * 16 * 512), BF)
        for hl in range(4):
            kv = slopes[hl] * kvals / scl
            khi, klo = hilo(kv)
            qv = -slopes[hl] * kvals / scl
            qhi, qlo = hilo(qv)
            al_l[0, hl * 8192:(hl + 1) * 8192] = BF(1.0)
            al_l[1, hl * 8192:(hl + 1) * 8192] = BF(1.0)
            al_l[2, hl * 8192:(hl + 1) * 8192] = khi
            al_l[3, hl * 8192:(hl + 1) * 8192] = klo
            al_r[0, hl * 8192:(hl + 1) * 8192] = qhi
            al_r[1, hl * 8192:(hl + 1) * 8192] = qlo
            al_r[2, hl * 8192:(hl + 1) * 8192] = BF(1.0)
            al_r[3, hl * 8192:(hl + 1) * 8192] = BF(1.0)
        d["alibi_lhsT"] = np.ascontiguousarray(al_l)
        d["alibi_rhs"] = np.ascontiguousarray(al_r)
        dm1 = np.zeros((4, 4, 128, 512), np.float64)
        for hl in range(4):
            for dj in range(4):
                delta = dj * 128 + ki_ - qi_
                dm1[hl, dj] = np.where(delta <= 0, np.exp(slopes[hl] * delta), 0.0)
        d["diag_masks1"] = np.ascontiguousarray(dm1.reshape(2048, 512).astype(np.float32)).astype(BF)
        gidx_all = np.zeros(16384, f)
        for r in range(8):
            br, jr = r // 4, r % 4
            tl = np.arange(2048)
            pch = np.where(tl < 1024, jr, 7 - jr)
            w = tl % 1024
            gidx_all[r * 2048:(r + 1) * 2048] = br * 8192 + pch * 1024 + w
        gi = gidx_all.reshape(128, 128)
        d["iota_ag_exp"] = np.ascontiguousarray(
            np.repeat(gi[:, :, None], 16, axis=2).reshape(128, 2048))
        gl = gidx_all[c * 2048:(c + 1) * 2048].reshape(16, 128).T
        d["iota_loc_exp"] = np.ascontiguousarray(
            np.repeat(gl[:, :, None], 16, axis=2).reshape(128, 256))
        d["av_off"] = np.array([[j * 512 * 2048]], dtype=np.int32)
        maps.append(d)
    return maps


def filter_map(maps, nc):
    names = set()
    for alloc in nc.m.functions[0].allocations:
        if isinstance(alloc, mybir.MemoryLocationSet) and alloc.kind == "ExternalInput":
            names.add(alloc.memorylocations[0].name)
    return [{k: v for k, v in m.items() if k in names} for m in maps]


def build_qkv(L):
    p = P()
    nc = p.nc
    NE = 12 if L == 0 else 6
    WE = NE * 128
    KE = 512 if L == 0 else 128
    NHV = 4 if L == 0 else 1
    xnf = nc.dram_tensor("xn_full", [4 * D, T], BF16, kind="ExternalInput")
    wT = nc.dram_tensor("l0_qkvT" if L == 0 else "l1_qkvT", [D, WE], F32, kind="ExternalInput")
    q_fm = nc.dram_tensor("q_fm", [512, SEQ], BF16, kind="ExternalOutput")
    k_fm = nc.dram_tensor("k_fm", [KE, SEQ], BF16, kind="ExternalOutput")
    v_blk = nc.dram_tensor("v_blk", [NHV * 64 * 128, 128], BF16, kind="ExternalOutput")

    wsb = p.sbuf("wsb", [128, 16 * WE], BF16)
    xnt = [p.sbuf(f"xnt{i}", [128, 16 * 512], BF16) for i in range(2)]
    est = [p.sbuf(f"est{i}", [128, 512], BF16) for i in range(4)]
    pq = [p.psum(f"pq{i}", [128, 512]) for i in range(2)]
    pv = [p.psum(f"pv{i}", [128, 512]) for i in range(2)]
    s_w = p.sem("s_w"); s_x = [p.sem("s_x0"), p.sem("s_x1")]; s_mm = p.sem("s_mm")
    s_ev = p.sem("s_ev"); s_st = [p.sem(f"s_st{i}") for i in range(4)]
    NEC = 8 if L == 0 else 5
    NTS = 4 if L == 0 else 4
    p.start()

    grp_per_tch = NEC + NTS
    st_per_tch = NEC + NTS * NHV

    @p.block.gpsimd
    def _(g):
        g.dma_start(out=wsb[:], in_=wT.ap().rearrange("(dc p) e -> p dc e", p=128)).then_inc(s_w, 16)

    @p.block.sync
    def _(sync):
        for tch in range(16):
            g0 = tch * 512
            r, col = span_loc(g0)
            if tch >= 2:
                sync.wait_ge(s_mm, grp_per_tch * (tch - 1))
            sync.dma_start(
                out=xnt[tch % 2][:],
                in_=xnf.ap()[r * D:(r + 1) * D, col:col + 512].rearrange("(dc p) t -> p dc t", p=128),
            ).then_inc(s_x[tch % 2], 16)

    @p.block.tensor
    def _(t):
        t.wait_ge(s_w, 16)
        for tch in range(16):
            t.wait_ge(s_x[tch % 2], 16 * (tch // 2 + 1))
            gi = 0
            for ec in range(NEC):
                pp = pq[ec % 2]
                if tch * grp_per_tch + gi >= 2:
                    t.wait_ge(s_ev, tch * grp_per_tch + gi - 1)
                for dc in range(16):
                    mi = t.matmul(pp[:], wsb[:, dc * WE + ec * 128: dc * WE + ec * 128 + 128],
                                  xnt[tch % 2][:, dc * 512:(dc + 1) * 512],
                                  start=(dc == 0), stop=(dc == 15))
                    if dc == 15:
                        mi.then_inc(s_mm, 1)
                gi += 1
            for ts in range(NTS):
                pp = pv[ts % 2]
                if tch * grp_per_tch + gi >= 2:
                    t.wait_ge(s_ev, tch * grp_per_tch + gi - 1)
                for dc in range(16):
                    mi = t.matmul(pp[:, 0:NHV * 128], xnt[tch % 2][:, dc * 512 + ts * 128: dc * 512 + ts * 128 + 128],
                                  wsb[:, dc * WE + NEC * 128: dc * WE + NEC * 128 + NHV * 128],
                                  start=(dc == 0), stop=(dc == 15))
                    if dc == 15:
                        mi.then_inc(s_mm, 1)
                gi += 1

    grp_is_v = ([False] * NEC + [True] * NTS) * 16
    ndma = [(NHV if g_ else 1) for g_ in grp_is_v]
    slot_cum = [[0], [0], [0], [0]]
    for idx, n in enumerate(ndma):
        for sl in range(4):
            slot_cum[sl].append(slot_cum[sl][-1] + (n if idx % 4 == sl else 0))

    @p.block.vector
    def _(v):
        for tch in range(16):
            gi = 0
            for ec in range(NEC):
                idx = tch * grp_per_tch + gi
                v.wait_ge(s_mm, idx + 1)
                if idx >= 4:
                    v.wait_ge(s_st[idx % 4], 16 * slot_cum[idx % 4][idx - 3])
                v.tensor_copy(est[idx % 4][:], pq[ec % 2][:]).then_inc(s_ev, 1)
                gi += 1
            for ts in range(NTS):
                idx = tch * grp_per_tch + gi
                v.wait_ge(s_mm, idx + 1)
                if idx >= 4:
                    v.wait_ge(s_st[idx % 4], 16 * slot_cum[idx % 4][idx - 3])
                v.tensor_copy(est[idx % 4][:, 0:NHV * 128], pv[ts % 2][:, 0:NHV * 128]).then_inc(s_ev, 1)
                gi += 1

    @p.block.scalar
    def _(sc):
        for tch in range(16):
            g0 = tch * 512
            gi = 0
            for ec in range(NEC):
                idx = tch * grp_per_tch + gi
                sc.wait_ge(s_ev, idx + 1)
                if L == 0:
                    dst = q_fm if ec < 4 else k_fm
                    row = (ec % 4) * 128
                else:
                    dst = q_fm if ec < 4 else k_fm
                    row = (ec * 128) if ec < 4 else 0
                sc.dma_start(out=dst.ap()[row:row + 128, g0:g0 + 512],
                             in_=est[idx % 4][:]).then_inc(s_st[idx % 4], 16)
                gi += 1
            for ts in range(NTS):
                idx = tch * grp_per_tch + gi
                sc.wait_ge(s_ev, idx + 1)
                kc = tch * 4 + ts
                for hl in range(NHV):
                    sc.dma_start(out=v_blk.ap()[(hl * 64 + kc) * 128:(hl * 64 + kc) * 128 + 128, :],
                                 in_=est[idx % 4][:, hl * 128:(hl + 1) * 128]).then_inc(s_st[idx % 4], 16)
                gi += 1
        for sl in range(4):
            sc.wait_ge(s_st[sl], 16 * slot_cum[sl][-1])

    p.close()
    return p.nc


def build_rope():
    p = P()
    nc = p.nc
    q_fm = nc.dram_tensor("q_fm", [512, SEQ], BF16, kind="ExternalInput")
    k_fm = nc.dram_tensor("k_fm", [128, SEQ], BF16, kind="ExternalInput")
    q_ro = nc.dram_tensor("q_ro", [512, SEQ], BF16, kind="ExternalOutput")
    k_ro = nc.dram_tensor("k_ro", [128, SEQ], BF16, kind="ExternalOutput")
    cosd = nc.dram_tensor("cos_fm", [64, SEQ], BF16, kind="ExternalInput")
    sind = nc.dram_tensor("sin_fm", [64, SEQ], BF16, kind="ExternalInput")

    hb = p.sbuf("hb", [128, SEQ], BF16)
    ob = p.sbuf("ob", [128, SEQ], BF16)
    cs = p.sbuf("cs", [128, SEQ], BF16)
    sn = p.sbuf("sn", [128, SEQ], BF16)
    t1 = p.sbuf("t1", [64, SEQ], BF16)
    t2 = p.sbuf("t2", [64, SEQ], BF16)
    s_ld = p.sem("s_ld"); s_ldc = p.sem("s_ldc"); s_o = p.sem("s_o"); s_st = p.sem("s_st")
    p.start()

    @p.block.sync
    def _(sync):
        sync.dma_start(out=cs[0:64, :], in_=cosd.ap()).then_inc(s_ldc, 16)
        sync.dma_start(out=cs[64:128, :], in_=cosd.ap()).then_inc(s_ldc, 16)
        sync.dma_start(out=sn[0:64, :], in_=sind.ap()).then_inc(s_ldc, 16)
        sync.dma_start(out=sn[64:128, :], in_=sind.ap()).then_inc(s_ldc, 16)
        for i in range(5):
            if i >= 1:
                sync.wait_ge(s_o, 2 * i)
            src = q_fm if i < 4 else k_fm
            row = (i % 4) * 128
            sync.dma_start(out=hb[:], in_=src.ap()[row:row + 128, :]).then_inc(s_ld, 16)
            sync.wait_ge(s_o, 2 * (i + 1))
            dst = q_ro if i < 4 else k_ro
            sync.dma_start(out=dst.ap()[row:row + 128, :], in_=ob[:]).then_inc(s_st, 16)
        sync.wait_ge(s_st, 16 * 5)

    @p.block.vector
    def _(v):
        v.wait_ge(s_ldc, 64)
        for i in range(5):
            v.wait_ge(s_ld, 16 * (i + 1))
            if i >= 1:
                v.wait_ge(s_st, 16 * i)
            p.guard(v, v.tensor_tensor(t1[:], hb[0:64, :], cs[0:64, :], ALU.mult))
            p.guard(v, v.tensor_tensor(t2[:], hb[64:128, :], sn[64:128, :], ALU.mult))
            v.tensor_tensor(ob[0:64, :], t1[:], t2[:], ALU.subtract).then_inc(s_o, 1)
            v.wait_ge(s_o, 2 * i + 1)
            p.guard(v, v.tensor_tensor(t1[:], hb[0:64, :], sn[0:64, :], ALU.mult))
            p.guard(v, v.tensor_tensor(t2[:], hb[64:128, :], cs[64:128, :], ALU.mult))
            v.tensor_tensor(ob[64:128, :], t1[:], t2[:], ALU.add).then_inc(s_o, 1)
            v.wait_ge(s_o, 2 * i + 2)

    p.close()
    return p.nc


def attn_schedule():
    sched = []
    for hl in range(4):
        for qb in range(16):
            chunks = [(kc, None) for kc in range(4 * qb)]
            chunks += [(4 * qb + dj, dj) for dj in range(4)]
            groups = [chunks[i:i + 3] for i in range(0, len(chunks), 3)]
            sched.append((hl, qb, groups))
    return sched


def build_attn(L):
    p = P()
    nc = p.nc
    KE = 512 if L == 0 else 128
    NHV = 4 if L == 0 else 1
    q_fm = nc.dram_tensor("q_ro" if L == 1 else "q_fm", [512, SEQ], BF16, kind="ExternalInput")
    k_fm = nc.dram_tensor("k_ro" if L == 1 else "k_fm", [KE, SEQ], BF16, kind="ExternalInput")
    v_blk = nc.dram_tensor("v_blk", [NHV * 64 * 128, 128], BF16, kind="ExternalInput")
    NDM = 4 if L == 0 else 16
    dmm = nc.dram_tensor("diag_masks" if L == 0 else "diag_masks1", [NDM * 128, 512], BF16, kind="ExternalInput")
    if L == 0:
        gwT = nc.dram_tensor("l0_gate_wT", [128, 128], F32, kind="ExternalInput")
        gb = nc.dram_tensor("l0_gate_b", [128, 1], F32, kind="ExternalInput")
    else:
        al_l = nc.dram_tensor("alibi_lhsT", [4, 4 * 64 * 128], BF16, kind="ExternalInput")
        al_r = nc.dram_tensor("alibi_rhs", [4, 4 * 16 * 512], BF16, kind="ExternalInput")
    av_full = nc.dram_tensor("av_full", [4 * D, T], BF16, kind="ExternalOutput")
    cc_in = nc.dram_tensor("cc_av_in", [D, T], BF16)
    cc_out = nc.dram_tensor("cc_av_out", [4 * D, T], BF16)

    q_sb = p.sbuf("q_sb", [128, SEQ], BF16)
    k_sb = p.sbuf("k_sb", [128, SEQ], BF16)
    v_sb = p.sbuf("v_sb", [128, 64 * 128], BF16)
    pt_sb = p.sbuf("pt_sb", [128, 2 * 1536], BF16)
    dm_sb = p.sbuf("dm_sb", [128, NDM * 512], BF16)
    rcp_sb = p.sbuf("rcp_sb", [1, 512], F32)
    rb_sb = p.sbuf("rb_sb", [128, 512], F32)
    avn_sb = [p.sbuf(f"avn{i}", [128, 512], BF16) for i in range(2)]
    negC = p.sbuf("negC", [128, 1], F32)
    sclt = p.sbuf("sclt", [128, 1], F32)
    ones_bf = p.sbuf("ones_bf", [128, 1], BF16)
    ones_f32 = p.sbuf("ones_f32", [1, 128], F32)
    if L == 0:
        gw_sb = p.sbuf("gw_sb", [128, 128], BF16)
        gb_sb = p.sbuf("gb_sb", [128, 1], F32)
        gt_sb = p.sbuf("gt_sb", [128, 512], BF16)
        grb_sb = p.sbuf("grb_sb", [128, 512], F32)
    else:
        all_sb = p.sbuf("all_sb", [4, 4 * 64 * 128], BF16)
        alr_sb = p.sbuf("alr_sb", [4, 4 * 16 * 512], BF16)

    ps_st = [p.psum(f"ps_st{i}", [128, 1536]) for i in range(2)]
    ps_av = p.psum("ps_av", [128, 512])
    ps_den = p.psum("ps_den", [1, 512])

    s_hlc = p.sem("s_hlc")
    s_hq = p.sem("s_hq"); s_hk = p.sem("s_hk"); s_hv = p.sem("s_hv")
    s_gw = p.sem("s_gw")
    s_stm = p.sem("s_stm")
    s_exp = p.sem("s_exp")
    s_msk = p.sem("s_msk")
    s_avr = p.sem("s_avr")
    s_rcp = p.sem("s_rcp"); s_rbm = p.sem("s_rbm"); s_rbc = p.sem("s_rbc")
    s_gmm = p.sem("s_gmm"); s_sig = p.sem("s_sig")
    s_avn = p.sem("s_avn"); s_avst = [p.sem("s_avst0"), p.sem("s_avst1")]
    s_cc = p.sem("s_cc"); s_fin = p.sem("s_fin")

    sched = attn_schedule()
    cumgrp = [0]
    for (_, _, groups) in sched:
        cumgrp.append(cumgrp[-1] + len(groups))
    NQBI = 64
    TOTG = cumgrp[-1]
    def hl_dmas(hl):
        return 3 if (L == 0 or hl == 0) else 1
    cum_hld = [0]
    for hl in range(4):
        cum_hld.append(cum_hld[-1] + hl_dmas(hl))
    p.start()

    @p.block.sync
    def _(sync):
        sync.dma_start(out=dm_sb[:], in_=dmm.ap().rearrange("(dj p) q -> p dj q", p=128)).then_inc(s_hlc, 16)
        if L == 1:
            sync.dma_start(out=all_sb[:], in_=al_l.ap()).then_inc(s_hlc, 16)
            sync.dma_start(out=alr_sb[:], in_=al_r.ap()).then_inc(s_hlc, 16)
        for hl in range(4):
            if hl >= 1:
                sync.wait_ge(s_avn, 16 * hl)
            sync.dma_start(out=q_sb[:], in_=q_fm.ap()[hl * 128:(hl + 1) * 128, :]).then_inc(s_hq, 16)
            if L == 0 or hl == 0:
                sync.dma_start(out=k_sb[:], in_=k_fm.ap()[(hl % NHV) * 128:(hl % NHV) * 128 + 128, :]).then_inc(s_hk, 16)
                sync.dma_start(out=v_sb[:],
                               in_=v_blk.ap()[(hl % NHV) * 64 * 128:((hl % NHV) + 1) * 64 * 128, :]
                               .rearrange("(kc pp) dd -> pp kc dd", pp=128)).then_inc(s_hv, 16)
            for qb in range(16):
                qbi = hl * 16 + qb
                sync.wait_ge(s_avn, qbi + 1)
                r, col = span_loc(qb * 512)
                row0 = r * 512 + hl * 128
                sync.dma_start(out=cc_in.ap()[row0:row0 + 128, col:col + 512],
                               in_=avn_sb[qbi % 2][:]).then_inc(s_avst[qbi % 2], 16)
        sync.wait_ge(s_cc, 1)
        sync.dma_start(out=av_full.ap(), in_=cc_out.ap()).then_inc(s_fin, 16)
        sync.wait_ge(s_fin, 16)

    @p.block.gpsimd
    def _(g):
        if L == 0:
            g.dma_start(out=gw_sb[:], in_=gwT.ap()).then_inc(s_gw, 16)
            g.dma_start(out=gb_sb[:], in_=gb.ap()).then_inc(s_gw, 16)
        g.wait_ge(s_avst[0], 16 * NQBI // 2)
        g.wait_ge(s_avst[1], 16 * NQBI // 2)
        g.collective_compute("AllGather", ALU.bypass, replica_groups=G4,
                             ins=[cc_in.ap().opt()], outs=[cc_out.ap().opt()]).then_inc(s_cc)

    CONST_LD = 16 if L == 0 else 48

    def _emit_av(t, qbi, hl, qb, groups, g_, G0):
        G = G0 + g_
        slot = g_ % 2
        t.wait_ge(s_exp, G + 1)
        grp = groups[g_]
        ndiag = sum(1 for (_, dj) in grp if dj is not None)
        if ndiag:
            last_dj = max(dj for (_, dj) in grp if dj is not None)
            t.wait_ge(s_msk, 4 * qbi + last_dj + 1)
        first_g = (g_ == 0)
        last_g = (g_ == len(groups) - 1)
        for ci, (kc, dj) in enumerate(grp):
            first = first_g and ci == 0
            last = last_g and ci == len(grp) - 1
            t.matmul(ps_av[:], v_sb[:, kc * 128:(kc + 1) * 128],
                     pt_sb[:, slot * 1536 + ci * 512: slot * 1536 + (ci + 1) * 512],
                     start=first, stop=last)
            mi = t.matmul(ps_den[:], ones_bf[:],
                          pt_sb[:, slot * 1536 + ci * 512: slot * 1536 + (ci + 1) * 512],
                          start=first, stop=last)
            if ci == len(grp) - 1:
                mi.then_inc(s_avr, 1)

    @p.block.tensor
    def _(t):
        t.wait_ge(s_hlc, CONST_LD)
        if L == 0:
            t.wait_ge(s_gw, 32)
        for qbi, (hl, qb, groups) in enumerate(sched):
            t.wait_ge(s_hq, 16 * (hl + 1))
            t.wait_ge(s_hk, 16 * ((hl + 1) if L == 0 else 1))
            t.wait_ge(s_hv, 16 * ((hl + 1) if L == 0 else 1))
            if qbi >= 1:
                t.wait_ge(s_rbc, qbi)
                if L == 0:
                    t.wait_ge(s_sig, qbi)
                t.wait_ge(s_avn, qbi)
            G0 = cumgrp[qbi]
            ng = len(groups)
            for g_ in range(ng):
                G = G0 + g_
                slot = g_ % 2
                if g_ >= 2:
                    t.wait_ge(s_exp, G - 1)
                for ci, (kc, dj) in enumerate(groups[g_]):
                    with_alibi = (L == 1 and dj is None)
                    mi = t.matmul(ps_st[slot][:, ci * 512:(ci + 1) * 512],
                                  k_sb[:, kc * 128:(kc + 1) * 128],
                                  q_sb[:, qb * 512:(qb + 1) * 512],
                                  start=True, stop=not with_alibi)
                    if with_alibi:
                        mi = t.matmul(ps_st[slot][:, ci * 512:(ci + 1) * 512],
                                      all_sb[:, hl * 8192 + kc * 128: hl * 8192 + (kc + 1) * 128],
                                      alr_sb[:, hl * 8192 + qb * 512: hl * 8192 + (qb + 1) * 512],
                                      start=False, stop=True)
                    if ci == len(groups[g_]) - 1:
                        mi.then_inc(s_stm, 1)
                if g_ >= 1:
                    _emit_av(t, qbi, hl, qb, groups, g_ - 1, G0)
            _emit_av(t, qbi, hl, qb, groups, ng - 1, G0)
            t.wait_ge(s_rcp, qbi + 1)
            t.matmul(ps_st[0][:, 0:512].bitcast(F32), ones_f32[:], rcp_sb[:],
                     start=True, stop=True).then_inc(s_rbm, 1)
            if L == 0:
                t.matmul(ps_st[1][:, 0:512].bitcast(F32), gw_sb[:],
                         q_sb[:, qb * 512:(qb + 1) * 512], start=True, stop=True).then_inc(s_gmm, 1)

    s_cst = p.sem("s_cst")

    @p.block.scalar
    def _(sc):
        sc.wait_ge(s_cst, 1)
        for qbi, (hl, qb, groups) in enumerate(sched):
            G0 = cumgrp[qbi]
            for g_ in range(len(groups)):
                G = G0 + g_
                slot = g_ % 2
                sc.wait_ge(s_stm, G + 1)
                if G >= 2:
                    sc.wait_ge(s_avr, G - 1)
                n = len(groups[g_])
                sc.activation(pt_sb[:, slot * 1536: slot * 1536 + n * 512],
                              ps_st[slot][:, 0:n * 512], AF.Exp,
                              bias=negC[:], scale=sclt[:]).then_inc(s_exp, 1)
            sc.wait_ge(s_rbm, qbi + 1)
            sc.copy(rb_sb[:], ps_st[0][:, 0:512].bitcast(F32)).then_inc(s_rbc, 1)
            if L == 0:
                sc.wait_ge(s_gmm, qbi + 1)
                sc.activation(gt_sb[:], ps_st[1][:, 0:512].bitcast(F32), AF.Sigmoid,
                              bias=gb_sb[:], scale=1.0).then_inc(s_sig, 1)

    @p.block.vector
    def _(v):
        v.memset(ones_bf[:], 1.0)
        v.memset(ones_f32[:], 1.0)
        v.memset(negC[:], -EXP_C)
        v.memset(sclt[:], SCALE).then_inc(s_cst, 1)
        for qbi, (hl, qb, groups) in enumerate(sched):
            G0 = cumgrp[qbi]
            for g_ in range(len(groups)):
                for ci, (kc, dj) in enumerate(groups[g_]):
                    if dj is None:
                        continue
                    G = G0 + g_
                    slot = g_ % 2
                    v.wait_ge(s_exp, G + 1)
                    sl = pt_sb[:, slot * 1536 + ci * 512: slot * 1536 + (ci + 1) * 512]
                    mi_ = dj if L == 0 else (hl * 4 + dj)
                    v.tensor_tensor(sl, sl, dm_sb[:, mi_ * 512:(mi_ + 1) * 512],
                                    ALU.mult).then_inc(s_msk, 1)
            v.wait_ge(s_avr, cumgrp[qbi + 1])
            v.reciprocal(rcp_sb[:], ps_den[0:1, :]).then_inc(s_rcp, 1)
            v.wait_ge(s_rbc, qbi + 1)
            if qbi >= 2:
                v.wait_ge(s_avst[qbi % 2], 16 * (qbi // 2))
            if L == 0:
                v.wait_ge(s_sig, qbi + 1)
                p.guard(v, v.tensor_tensor(grb_sb[:], gt_sb[:], rb_sb[:], ALU.mult))
                v.tensor_tensor(avn_sb[qbi % 2][:], ps_av[:], grb_sb[:], ALU.mult).then_inc(s_avn, 1)
            else:
                v.tensor_tensor(avn_sb[qbi % 2][:], ps_av[:], rb_sb[:], ALU.mult).then_inc(s_avn, 1)

    p.close()
    return p.nc


def build_ffn0(upto="full"):
    p = P()
    nc = p.nc
    xnl = nc.dram_tensor("xn_loc", [D, T], BF16, kind="ExternalInput")
    rwT = nc.dram_tensor("l0_router_w", [D, 16], F32, kind="ExternalInput")
    rbr = nc.dram_tensor("l0_router_b", [1, 16], F32, kind="ExternalInput")
    ewT = nc.dram_tensor("l0_expert_w", [D, D], F32, kind="ExternalInput")
    ebr = nc.dram_tensor("l0_expert_b", [128, 16], F32, kind="ExternalInput")
    iag = nc.dram_tensor("iota_ag_exp", [128, 2048], F32, kind="ExternalInput")
    ilo = nc.dram_tensor("iota_loc_exp", [128, 256], F32, kind="ExternalInput")
    idf = nc.dram_tensor("ident_f32", [128, 128], F32, kind="ExternalInput")
    g_fm = nc.dram_tensor("g_fm", [D, T], F32, kind="ExternalOutput")
    dbg = nc.dram_tensor("dbg", [128, 256], F32, kind="ExternalOutput") if upto != "full" else None
    dbg2 = nc.dram_tensor("dbg2", [1, 2048], F32, kind="ExternalOutput") if upto != "full" else None
    cc_in = nc.dram_tensor("cc_S_in", [T // 8, 128], F32)
    cc_out = nc.dram_tensor("cc_S_out", [T, 128], F32)
    c_dram = nc.dram_tensor("c_scr", [1, T], F32)

    xn_sb = p.sbuf("xn_sb", [128, 16 * T], BF16)
    ew_sb = p.sbuf("ew_sb", [128, 16 * D], BF16)
    rw_sb = p.sbuf("rw_sb", [128, 16 * 16], BF16)
    rb_sb = p.sbuf("rb_sb", [1, 16], BF16)
    S_loc = p.sbuf("S_loc", [128, 256], F32)
    S_sb = p.sbuf("S_sb", [128, 2048], F32)
    iag_sb = p.sbuf("iag_sb", [128, 2048], F32)
    ilo_sb = p.sbuf("ilo_sb", [128, 256], F32)
    eb_sb = p.sbuf("eb_sb", [128, 16], F32)
    exp_sb = p.sbuf("exp_sb", [128, 16], F32)
    srow = p.sbuf("srow", [128, 8], F32)
    loB = [p.sbuf(f"lo_sb{i}", [1, 16], F32) for i in range(2)]
    hiB = [p.sbuf(f"hi_sb{i}", [1, 16], F32) for i in range(2)]
    vhi_sb = p.sbuf("vhi_sb", [1, 16], F32)
    mid_sb = p.sbuf("mid_sb", [1, 16], F32)
    mide = p.sbuf("mide", [1, 2048], F32)
    cnt_sb = p.sbuf("cnt_sb", [1, 16], F32)
    cge_sb = p.sbuf("cge_sb", [1, 16], I32)
    need_sb = p.sbuf("need_sb", [1, 16], F32)
    ge_sb = p.sbuf("ge_sb", [128, 2048], F32)
    eq_sb = p.sbuf("eq_sb", [128, 2048], F32)
    lt_sb = p.sbuf("lt_sb", [128, 2048], F32)
    part_bf = p.sbuf("part_bf", [128, 16], BF16)
    msk_sb = p.sbuf("msk_sb", [128, 256], F32)
    eqloc_sb = p.sbuf("eqloc_sb", [128, 256], F32)
    ltloc_sb = p.sbuf("ltloc_sb", [128, 256], F32)
    c_tok = p.sbuf("c_tok", [128, 16], F32)
    c_t2 = p.sbuf("c_t2", [16, 128], F32)
    idf_sb = p.sbuf("idf_sb", [128, 128], F32)
    c_row = p.sbuf("c_row", [1, T], F32)
    ones_bf = p.sbuf("ones_bf", [128, 1], BF16)
    ones_f32 = p.sbuf("ones_f32", [1, 128], F32)
    onesr_bf = p.sbuf("onesr_bf", [1, 128], BF16)
    ebias_sb = p.sbuf("ebias_sb", [128, 512], F32)
    y_sb = p.sbuf("y_sb", [128, 512], F32)
    g_sb = [p.sbuf(f"g_sb{i}", [128, 512], F32) for i in range(2)]

    ps_r = p.psum("ps_r", [128, 16])
    ps_b = p.psum("ps_b", [128, 2048])
    ps_e = [p.psum(f"ps_e{i}", [128, 512]) for i in range(2)]

    s_ld = p.sem("s_ld"); s_gld = p.sem("s_gld"); s_rt = p.sem("s_rt"); s_sm = p.sem("s_sm")
    s_sx = p.sem("s_sx"); s_sl = p.sem("s_sl"); s_sst = p.sem("s_sst")
    s_cc = p.sem("s_cc"); s_bt = p.sem("s_bt"); s_bv = p.sem("s_bv")
    s_cdma = p.sem("s_cdma"); s_ct = p.sem("s_ct"); s_em = p.sem("s_em"); s_ee = p.sem("s_ee")
    s_cbm = p.sem("s_cbm"); s_y = p.sem("s_y"); s_g = p.sem("s_g")
    s_gst = [p.sem("s_gst0"), p.sem("s_gst1")]; s_cst0 = p.sem("s_cst0")

    NB = 44
    NTB = 16
    p.start()

    @p.block.sync
    def _(sync):
        sync.dma_start(out=xn_sb[:], in_=xnl.ap().rearrange("(dc pp) t -> pp dc t", pp=128)).then_inc(s_ld, 16)
        sync.dma_start(out=iag_sb[:], in_=iag.ap()).then_inc(s_ld, 16)
        sync.dma_start(out=ilo_sb[:], in_=ilo.ap()).then_inc(s_ld, 16)
        sync.dma_start(out=eb_sb[:], in_=ebr.ap()).then_inc(s_ld, 16)
        sync.dma_start(out=idf_sb[:], in_=idf.ap()).then_inc(s_ld, 16)
        for i in range(16):
            sync.wait_ge(s_sl, i + 1)
            sync.dma_start(out=cc_in.ap().rearrange("r (a e) -> (r a) e", e=16)[i * 128:(i + 1) * 128, :],
                           in_=S_loc[:, i * 16:(i + 1) * 16]).then_inc(s_sst, 16)
        if upto == "softmax":
            sync.wait_ge(s_sst, 256)
            sync.dma_start(out=dbg.ap(), in_=S_loc[:]).then_inc(s_gst, 16)
            sync.wait_ge(s_gst, 16)
            return
        sync.wait_ge(s_cc, 1)
        sync.dma_start(out=S_sb[:], in_=cc_out.ap().rearrange("(pp i8) w -> pp (i8 w)", i8=16)).then_inc(s_ld, 16)
        sync.wait_ge(s_cdma, 2)
        sync.dma_start(out=bass.AP(c_dram, 0, [[128, 16], [1, 128]]), in_=c_t2[:]).then_inc(s_cdma, 16)
        sync.wait_ge(s_cdma, 18)
        sync.dma_start(out=c_row[:], in_=c_dram.ap()).then_inc(s_cdma, 16)
        if upto == "full":
            for ti in range(64):
                sync.wait_ge(s_g, ti + 1)
                ec, tch = ti // 4, ti % 4
                sync.dma_start(out=g_fm.ap()[ec * 128:(ec + 1) * 128, tch * 512:(tch + 1) * 512],
                               in_=g_sb[ti % 2][:]).then_inc(s_gst[ti % 2], 16)
            sync.wait_ge(s_gst[0], 16 * 32)
            sync.wait_ge(s_gst[1], 16 * 32)
        elif upto == "router":
            sync.dma_start(out=dbg.ap(), in_=S_loc[:]).then_inc(s_gst[0], 16)
            sync.dma_start(out=dbg2.ap(), in_=S_sb[0:1, :]).then_inc(s_gst[1], 16)
            sync.wait_ge(s_gst[0], 16)
            sync.wait_ge(s_gst[1], 16)
        else:
            sync.wait_ge(s_cdma, 34)
            sync.dma_start(out=dbg.ap(), in_=S_loc[:]).then_inc(s_gst[0], 16)
            sync.dma_start(out=dbg2.ap()[:, 0:16], in_=vhi_sb[:]).then_inc(s_gst[0], 16)
            sync.dma_start(out=dbg2.ap()[:, 16:32], in_=hiB[NTB % 2][:]).then_inc(s_gst[0], 16)
            sync.dma_start(out=dbg2.ap()[:, 32:48], in_=need_sb[:]).then_inc(s_gst[0], 16)
            sync.dma_start(out=dbg2.ap()[:, 48:2048], in_=c_row[:, 0:2000]).then_inc(s_gst[0], 16)
            sync.wait_ge(s_gst[0], 16 * 5)

    @p.block.gpsimd
    def _(g):
        g.dma_start(out=rw_sb[:], in_=rwT.ap().rearrange("(dc pp) e -> pp dc e", pp=128)).then_inc(s_gld, 16)
        g.dma_start(out=rb_sb[:], in_=rbr.ap()).then_inc(s_gld, 16)
        g.dma_start(out=ew_sb[:], in_=ewT.ap().rearrange("(hc pp) e -> pp hc e", pp=128)).then_inc(s_gld, 16)
        if upto == "softmax":
            return
        g.wait_ge(s_sst, 16 * 16)
        g.collective_compute("AllGather", ALU.bypass, replica_groups=G8,
                             ins=[cc_in.ap().opt()], outs=[cc_out.ap().opt()]).then_inc(s_cc)

    NPHASE = 2 * (NB + 1) + 2 * NTB + 2

    @p.block.tensor
    def _(t):
        t.wait_ge(s_ld, 16 * 5)
        t.wait_ge(s_gld, 16 * 3)
        t.wait_ge(s_cst0, 1)
        for i in range(16):
            if i >= 1:
                t.wait_ge(s_sx, i)
            for dc in range(16):
                t.matmul(ps_r[:], xn_sb[:, dc * T + i * 128: dc * T + i * 128 + 128],
                         rw_sb[:, dc * 16:(dc + 1) * 16], start=(dc == 0), stop=False)
            t.matmul(ps_r[:], onesr_bf[:], rb_sb[:], start=False, stop=True).then_inc(s_rt, 1)
        if upto in ("router", "softmax"):
            return
        t.wait_ge(s_ld, 16 * 6)
        t.wait_ge(s_sx, 16)
        nbv = 0; nbt = 0
        for it in range(NB + 1 + NTB):
            nbv += 1
            t.wait_ge(s_bv, nbv)
            for jj in range(4):
                mi = t.matmul(ps_b[:, jj * 512:(jj + 1) * 512], ones_f32[:], mide[:, jj * 512:(jj + 1) * 512],
                              start=True, stop=True)
            mi.then_inc(s_bt, 1); nbt += 1
            nbv += 1
            t.wait_ge(s_bv, nbv)
            t.matmul(ps_r[0:1, :], ones_bf[:], part_bf[:], start=True, stop=True).then_inc(s_bt, 1); nbt += 1
        for z in range(2):
            nbv += 1
            t.wait_ge(s_bv, nbv)
            t.matmul(ps_b[:, 0:256], ones_f32[:], mide[:, 0:256], start=True, stop=True).then_inc(s_bt, 1)
            nbt += 1
        t.wait_ge(s_cdma, 1)
        t.transpose(ps_e[0][0:16, 0:128], c_tok[:], idf_sb[:]).then_inc(s_ct, 1)
        if upto != "full":
            return
        t.wait_ge(s_cdma, 34)
        for ti in range(64):
            ec, tch = ti // 4, ti % 4
            if ti >= 2:
                t.wait_ge(s_ee, ti - 1)
            for dc in range(16):
                mi = t.matmul(ps_e[ti % 2][:], ew_sb[:, dc * D + ec * 128: dc * D + ec * 128 + 128],
                              xn_sb[:, dc * T + tch * 512: dc * T + (tch + 1) * 512],
                              start=(dc == 0), stop=(dc == 15))
                if dc == 15:
                    mi.then_inc(s_em, 1)
            if ti >= 1:
                t.wait_ge(s_y, ti)
            t.matmul(ps_b[:, 0:512], ones_f32[:], c_row[:, tch * 512:(tch + 1) * 512],
                     start=True, stop=True).then_inc(s_cbm, 1)

    @p.block.scalar
    def _(sc):
        for i in range(16):
            sc.wait_ge(s_rt, i + 1)
            sc.wait_ge(s_sm, 2 * (i + 1))
            sc.activation(exp_sb[:], ps_r[:], AF.Exp, bias=srow[:, 1:2], scale=1.0,
                          accum_out=srow[:, 2:3]).then_inc(s_sx, 1)
        if upto != "full":
            return
        for ti in range(64):
            ec, tch = ti // 4, ti % 4
            sc.wait_ge(s_em, ti + 1)
            sc.activation(ebias_sb[:], ps_e[ti % 2][:], AF.Identity,
                          bias=eb_sb[:, ec:ec + 1], scale=1.0).then_inc(s_ee, 1)
            sc.wait_ge(s_y, ti + 1)
            if ti >= 2:
                sc.wait_ge(s_gst[ti % 2], 16 * (ti // 2))
            sc.activation(g_sb[ti % 2][:], y_sb[:], AF.Silu).then_inc(s_g, 1)

    @p.block.vector
    def _(v):
        v.memset(ones_bf[:], 1.0)
        v.memset(onesr_bf[:], 1.0)
        v.memset(ones_f32[:], 1.0)
        v.memset(loB[0][:], 0.0)
        v.memset(hiB[0][:], 1.0).then_inc(s_cst0, 1)
        for i in range(16):
            v.wait_ge(s_rt, i + 1)
            p.guard(v, v.reduce_max(srow[:, 0:1], ps_r[:], axis=AX.X))
            v.tensor_scalar_mul(srow[:, 1:2], srow[:, 0:1], -1.0).then_inc(s_sm, 2)
            v.wait_ge(s_sx, i + 1)
            p.guard(v, v.reciprocal(srow[:, 3:4], srow[:, 2:3]))
            v.tensor_scalar_mul(S_loc[:, i * 16:(i + 1) * 16], exp_sb[:], srow[:, 3:4]).then_inc(s_sl, 1)
        if upto in ("router", "softmax"):
            return
        nbv = 0; nbt = 0
        mid_bcast_ap = bass.AP(mid_sb.ap().tensor, 0, [[16, 1], [0, 128], [1, 16]])
        for it in range(NB + 1 + NTB):
            tie = it > NB
            pp = it % 2 if it < NB else (it - NB - 1) % 2
            qq = 1 - pp
            if it < NB:
                p.guard(v, v.tensor_add(mid_sb[:], loB[pp][:], hiB[pp][:]))
                p.guard(v, v.tensor_scalar_mul(mid_sb[:], mid_sb[:], 0.5))
            elif it == NB:
                p.guard(v, v.tensor_copy(mid_sb[:], hiB[NB % 2][:]))
            else:
                p.guard(v, v.tensor_add(mid_sb[:], loB[pp][:], hiB[pp][:]))
                p.guard(v, v.tensor_scalar_mul(mid_sb[:], mid_sb[:], 0.5))
            v.tensor_copy(mide.ap().rearrange("o (i e) -> o i e", e=16), mid_bcast_ap).then_inc(s_bv, 1); nbv += 1
            nbt += 1
            v.wait_ge(s_bt, nbt)
            if not tie:
                p.guard(v, v.tensor_tensor(ge_sb[:], S_sb[:], ps_b[:], ALU.is_gt))
                with p.nc.allow_low_precision(reason="counts<=128 exact in bf16"):
                    v.reduce_sum(part_bf[:], ge_sb.ap().rearrange("pp (i e) -> pp e i", e=16),
                                 axis=AX.X).then_inc(s_bv, 1)
            else:
                p.guard(v, v.tensor_tensor(lt_sb[:], iag_sb[:], ps_b[:], ALU.is_lt))
                p.guard(v, v.tensor_tensor(lt_sb[:], lt_sb[:], eq_sb[:], ALU.mult))
                with p.nc.allow_low_precision(reason="counts<=128 exact in bf16"):
                    v.reduce_sum(part_bf[:], lt_sb.ap().rearrange("pp (i e) -> pp e i", e=16),
                                 axis=AX.X).then_inc(s_bv, 1)
            nbv += 1
            nbt += 1
            v.wait_ge(s_bt, nbt)
            p.guard(v, v.tensor_copy(cnt_sb[:], ps_r[0:1, :]))
            if it < NB:
                p.guard(v, v.tensor_scalar(cge_sb[:], cnt_sb[:], 1024.0, None, ALU.is_ge))
                p.guard(v, v.select(loB[qq][:], cge_sb[:], mid_sb[:], loB[pp][:], add_drain=True))
                p.guard(v, v.select(hiB[qq][:], cge_sb[:], hiB[pp][:], mid_sb[:], add_drain=True))
            elif it == NB:
                p.guard(v, v.tensor_scalar(need_sb[:], cnt_sb[:], -1.0, 1024.0, ALU.mult, ALU.add))
                p.guard(v, v.tensor_tensor(eq_sb[:], S_sb[:], ps_b[:], ALU.is_equal))
                p.guard(v, v.tensor_copy(vhi_sb[:], hiB[NB % 2][:]))
                p.guard(v, v.memset(loB[0][:], 0.0))
                p.guard(v, v.memset(hiB[0][:], 16384.0))
            else:
                p.guard(v, v.tensor_tensor(cge_sb[:], cnt_sb[:], need_sb[:], ALU.is_ge))
                p.guard(v, v.select(hiB[qq][:], cge_sb[:], mid_sb[:], hiB[pp][:], add_drain=True))
                p.guard(v, v.select(loB[qq][:], cge_sb[:], loB[pp][:], mid_sb[:], add_drain=True))
        for z in range(2):
            src = vhi_sb if z == 0 else hiB[NTB % 2]
            src_ap = bass.AP(src.ap().tensor, 0, [[16, 1], [0, 16], [1, 16]])
            v.tensor_copy(mide[:, 0:256].rearrange("o (i e) -> o i e", e=16), src_ap).then_inc(s_bv, 1)
            nbv += 1
            nbt += 1
            v.wait_ge(s_bt, nbt)
            if z == 0:
                p.guard(v, v.tensor_tensor(msk_sb[:], S_loc[:], ps_b[:, 0:256], ALU.is_gt))
                p.guard(v, v.tensor_tensor(eqloc_sb[:], S_loc[:], ps_b[:, 0:256], ALU.is_equal))
            else:
                p.guard(v, v.tensor_tensor(ltloc_sb[:], ilo_sb[:], ps_b[:, 0:256], ALU.is_lt))
                p.guard(v, v.tensor_tensor(eqloc_sb[:], eqloc_sb[:], ltloc_sb[:], ALU.mult))
                p.guard(v, v.tensor_tensor(msk_sb[:], msk_sb[:], eqloc_sb[:], ALU.add))
                p.guard(v, v.tensor_tensor(msk_sb[:], msk_sb[:], S_loc[:], ALU.mult))
                v.reduce_sum(c_tok[:], msk_sb.ap().rearrange("pp (i e) -> pp i e", e=16),
                             axis=AX.X).then_inc(s_cdma, 1)
                v.wait_ge(s_ct, 1)
                v.tensor_copy(c_t2[:], ps_e[0][0:16, 0:128]).then_inc(s_cdma, 1)
        if upto == "full":
            for ti in range(64):
                v.wait_ge(s_ee, ti + 1)
                v.wait_ge(s_cbm, ti + 1)
                if ti >= 1:
                    v.wait_ge(s_g, ti)
                v.tensor_tensor(y_sb[:], ebias_sb[:], ps_b[:, 0:512], ALU.mult).then_inc(s_y, 1)

    p.close()
    return p.nc


def build_ep(L):
    p = P()
    nc = p.nc
    avf = nc.dram_tensor("av_full", [4 * D, T], BF16, kind="ExternalInput")
    woT = nc.dram_tensor("l0_owT" if L == 0 else "l1_wo", [D, D], F32, kind="ExternalInput")
    gfm = nc.dram_tensor("g_fm", [D, T], F32, kind="ExternalInput")
    xin = nc.dram_tensor("x_in", [T, D], F32, kind="ExternalInput")
    offt = nc.dram_tensor("av_off", [1, 1], I32, kind="ExternalInput")
    idf = nc.dram_tensor("ident_f32", [128, 128], F32, kind="ExternalInput")
    xout = nc.dram_tensor("x_out", [T, D], F32, kind="ExternalOutput")

    av_sb = p.sbuf("av_sb", [128, 16 * T], BF16)
    wo_sb = p.sbuf("wo_sb", [128, 16 * D], BF16)
    gt_sb = [p.sbuf(f"gt{i}", [128, 512], F32) for i in range(2)]
    sg_sb = p.sbuf("sg_sb", [128, 512], F32)
    sgT_sb = p.sbuf("sgT_sb", [128, 4 * D], F32)
    xt_sb = [p.sbuf(f"xts{i}", [128, D], F32) for i in range(2)]
    xo_sb = [p.sbuf(f"xo{i}", [128, D], F32) for i in range(2)]
    idf_sb = p.sbuf("idf_sb", [128, 128], F32)
    off_sb = p.sbuf("off_sb", [1, 1], I32)

    ps_o = [p.psum(f"ps_o{i}", [128, 512]) for i in range(2)]
    ps_t = [p.psum(f"ps_t{i}", [128, 128]) for i in range(2)]

    s_ld = p.sem("s_ld"); s_gl2 = p.sem("s_gl2"); s_av = p.sem("s_av")
    s_om = p.sem("s_om"); s_sg = p.sem("s_sg"); s_tp = p.sem("s_tp")
    s_te = p.sem("s_te"); s_xld = [p.sem("s_xld0"), p.sem("s_xld1")]; s_xo = p.sem("s_xo")
    s_st = [p.sem("s_stp0"), p.sem("s_stp1")]; s_gld = [p.sem("s_gld0"), p.sem("s_gld1")]
    p.start()

    @p.block.gpsimd
    def _(g):
        g.dma_start(out=off_sb[:], in_=offt.ap()).then_inc(s_gl2, 16)
        g.dma_start(out=wo_sb[:], in_=woT.ap().rearrange("(dc pp) e -> pp dc e", pp=128)).then_inc(s_gl2, 16)
        g.wait_ge(s_gl2, 32)
        with g.register("goff") as goff, g.register("gr2") as gr2:
            g.reg_load(goff, off_sb[:1, :1])
            for hc in range(16):
                r, hl = hc // 4, hc % 4
                fixed = (r * 2048 + hl * 128) * 2048
                g.reg_add(gr2, goff, fixed)
                g.dma_start(out=av_sb[:, hc * T:(hc + 1) * T],
                            in_=bass.AP(avf, gr2, [[2048, 128], [1, 2048]])).then_inc(s_av, 16)

    @p.block.sync
    def _(sync):
        sync.dma_start(out=idf_sb[:], in_=idf.ap()).then_inc(s_ld, 16)
        for tch in range(4):
            for ec in range(16):
                ti = tch * 16 + ec
                if ti >= 2:
                    sync.wait_ge(s_sg, ti - 1)
                sync.dma_start(out=gt_sb[ti % 2][:],
                               in_=gfm.ap()[ec * 128:(ec + 1) * 128, tch * 512:(tch + 1) * 512]).then_inc(s_gld[ti % 2], 16)
            for sub in range(4):
                si = tch * 4 + sub
                if si >= 2:
                    sync.wait_ge(s_xo, si - 1)
                sync.dma_start(out=xt_sb[si % 2][:],
                               in_=xin.ap()[si * 128:(si + 1) * 128, :]).then_inc(s_xld[si % 2], 16)
                sync.wait_ge(s_xo, si + 1)
                sync.dma_start(out=xout.ap()[si * 128:(si + 1) * 128, :],
                               in_=xo_sb[si % 2][:]).then_inc(s_st[si % 2], 16)
        sync.wait_ge(s_st[0], 16 * 8)
        sync.wait_ge(s_st[1], 16 * 8)

    @p.block.tensor
    def _(t):
        t.wait_ge(s_ld, 16)
        t.wait_ge(s_gl2, 32)
        t.wait_ge(s_av, 16 * 16)
        for tch in range(4):
            for ec in range(16):
                ti = tch * 16 + ec
                if ti >= 2:
                    t.wait_ge(s_sg, ti - 1)
                for dc in range(16):
                    mi = t.matmul(ps_o[ti % 2][:], wo_sb[:, dc * D + ec * 128: dc * D + ec * 128 + 128],
                                  av_sb[:, dc * T + tch * 512: dc * T + (tch + 1) * 512],
                                  start=(dc == 0), stop=(dc == 15))
                    if dc == 15:
                        mi.then_inc(s_om, 1)
                t.wait_ge(s_sg, ti + 1)
                for sub in range(4):
                    tpi = ti * 4 + sub
                    if tpi >= 2:
                        t.wait_ge(s_te, tpi - 1)
                    t.transpose(ps_t[tpi % 2][:], sg_sb[:, sub * 128:(sub + 1) * 128],
                                idf_sb[:]).then_inc(s_tp, 1)

    @p.block.vector
    def _(v):
        for tch in range(4):
            for ec in range(16):
                ti = tch * 16 + ec
                v.wait_ge(s_om, ti + 1)
                v.wait_ge(s_gld[ti % 2], 16 * (ti // 2 + 1))
                if ti >= 1:
                    v.wait_ge(s_tp, 4 * ti)
                if ec == 0 and tch >= 1:
                    v.wait_ge(s_xo, 4 * tch)
                v.tensor_tensor(sg_sb[:], ps_o[ti % 2][:], gt_sb[ti % 2][:], ALU.mult).then_inc(s_sg, 1)
                for sub in range(4):
                    tpi = ti * 4 + sub
                    v.wait_ge(s_tp, tpi + 1)
                    v.tensor_copy(sgT_sb[:, sub * D + ec * 128: sub * D + ec * 128 + 128],
                                  ps_t[tpi % 2][:]).then_inc(s_te, 1)
            for sub in range(4):
                si = tch * 4 + sub
                v.wait_ge(s_xld[si % 2], 16 * (si // 2 + 1))
                if si >= 2:
                    v.wait_ge(s_st[si % 2], 16 * (si // 2))
                v.tensor_tensor(xo_sb[si % 2][:], xt_sb[si % 2][:], sgT_sb[:, sub * D:(sub + 1) * D],
                                ALU.add).then_inc(s_xo, 1)

    p.close()
    return p.nc


def build_ffn1():
    p = P()
    nc = p.nc
    xnl = nc.dram_tensor("xn_loc", [D, T], BF16, kind="ExternalInput")
    wgT = nc.dram_tensor("l1_wgT", [D, 8], F32, kind="ExternalInput")
    w1T = nc.dram_tensor("l1_w1T", [D, 7168], F32, kind="ExternalInput")
    w3T = nc.dram_tensor("l1_w3T", [D, 7168], F32, kind="ExternalInput")
    w2T = nc.dram_tensor("l1_w2T", [7168, D], F32, kind="ExternalInput")
    a1T = nc.dram_tensor("l1_a1T", [D, 256], F32, kind="ExternalInput")
    a3T = nc.dram_tensor("l1_a3T", [D, 256], F32, kind="ExternalInput")
    a2T = nc.dram_tensor("l1_a2T", [7168, 256], F32, kind="ExternalInput")
    b1T = nc.dram_tensor("l1_b1T", [256, 7168], F32, kind="ExternalInput")
    b3T = nc.dram_tensor("l1_b3T", [256, 7168], F32, kind="ExternalInput")
    b2T = nc.dram_tensor("l1_b2T", [256, D], F32, kind="ExternalInput")
    idb = nc.dram_tensor("ident_bf", [128, 128], BF16, kind="ExternalInput")
    idf = nc.dram_tensor("ident_f32", [128, 128], F32, kind="ExternalInput")
    g_fm = nc.dram_tensor("g_fm", [D, T], F32, kind="ExternalOutput")
    x2d = [nc.dram_tensor(f"x2_scr{k}", [7168, T], BF16) for k in range(2)]
    w2bf = nc.dram_tensor("w2bf_scr", [7168, D], BF16)
    rwd = nc.dram_tensor("rw_scr", [2, T], F32)

    FC = 56
    arena = p.sbuf("arena", [128, 61440], BF16)
    xn_sb = arena[:, 0:32768]
    ma_sb = arena[:, 32768:49152]
    x2r = [arena[:, 0:28672], arena[:, 32768:61440]]
    maskf = p.sbuf("maskf", [128, 4 * T], BF16)
    probs = p.sbuf("probs1", [128, 16 * 8], F32)
    m8t = p.sbuf("m8t", [128, 8], F32)
    rep1 = p.sbuf("rep1", [128, 8], F32)
    rep2 = p.sbuf("rep2", [128, 8], F32)
    scr1 = p.sbuf("scr1", [128, 8], F32)
    scr2 = p.sbuf("scr2", [128, 8], F32)
    mk1 = p.sbuf("mk1", [128, 8], F32)
    mk2 = p.sbuf("mk2", [128, 8], F32)
    m256 = p.sbuf("m256", [128, 256], BF16)
    srow = p.sbuf("srow1", [128, 8], F32)
    exp_sb = p.sbuf("exp1", [128, 8], F32)
    rw_tok = p.sbuf("rw_tok", [128, 32], F32)
    rwT_sb = p.sbuf("rwT_sb", [32, 128], F32)
    rwrt = [p.sbuf(f"rwrt{k}", [1, 512], F32) for k in range(2)]
    rwb2 = [p.sbuf(f"rwb2_{k}", [128, 512], F32) for k in range(2)]
    wg_sb = p.sbuf("wg_sb", [128, 16 * 8], BF16)
    idb_sb = p.sbuf("idb1_sb", [128, 128], BF16)
    idf_sb = p.sbuf("idf1_sb", [128, 128], F32)
    aT_sb = p.sbuf("aT_sb", [128, 16 * 256], BF16)
    wst = [p.sbuf(f"wst{i}", [128, 16 * 128], BF16) for i in range(4)]
    bst = [p.sbuf(f"bst{i}", [128, 512], BF16) for i in range(2)]
    w2st = [p.sbuf(f"w2st{i}", [128, 256], BF16) for i in range(2)]
    w2pc = p.sbuf("w2pc", [128, 2048], BF16)
    a2st = [p.sbuf(f"a2st{i}", [128, 256], BF16) for i in range(2)]
    b2_sb = p.sbuf("b2_sb", [128, 2 * 2048], BF16)
    b1t = p.sbuf("b1t", [128, 512], BF16)
    b3t = p.sbuf("b3t", [128, 512], BF16)
    x1s = p.sbuf("x1s", [128, 512], BF16)
    x2t = [p.sbuf(f"x2t{i}", [128, 512], BF16) for i in range(2)]
    ma2 = p.sbuf("ma2", [128, 4 * 512], BF16)
    out2 = [p.sbuf(f"out2_{i}", [128, 512], F32) for i in range(2)]
    tmp_sb = p.sbuf("tmp1_sb", [128, 512], F32)
    g_sb = [p.sbuf(f"g1_{i}", [128, 512], F32) for i in range(2)]
    ones_f32 = p.sbuf("ones1_f32", [1, 128], F32)

    ps8 = [p.psum(f"ps8_{i}", [128, 512]) for i in range(8)]

    s_xl = p.sem("s_xl"); s_cl = p.sem("s_cl"); s_gl = p.sem("s_gl")
    s_rt = p.sem("s_rt"); s_sm = p.sem("s_sm"); s_sx = p.sem("s_sx")
    s_pr = p.sem("s_pr"); s_m256 = p.sem("s_m256"); s_tp = p.sem("s_tp")
    s_mf = p.sem("s_mf"); s_rwt = p.sem("s_rwt"); s_rwte = p.sem("s_rwte")
    s_rwd = p.sem("s_rwd")
    s_am = p.sem("s_am"); s_ae = p.sem("s_ae"); s_a3l = p.sem("s_a3l")
    s_ws = [p.sem(f"s_ws{i}") for i in range(2)]
    s_bs = [p.sem(f"s_bs{i}") for i in range(2)]
    s_bm = p.sem("s_bm"); s_be = p.sem("s_be"); s_x1 = p.sem("s_x1")
    s_sil = p.sem("s_sil"); s_x2 = p.sem("s_x2")
    s_x2st = [p.sem(f"s_x2st{i}") for i in range(2)]
    s_w2p = [p.sem(f"s_w2p{i}") for i in range(2)]
    s_w2w = [p.sem(f"s_w2w{i}") for i in range(2)]
    s_x2l = [p.sem(f"s_x2l{i}") for i in range(2)]
    s_a2l = [p.sem(f"s_a2l{i}") for i in range(2)]
    s_a2mm = p.sem("s_a2mm"); s_a2m = p.sem("s_a2m"); s_a2e = p.sem("s_a2e")
    s_w2l = [p.sem(f"s_w2l{i}") for i in range(2)]
    s_w2c = p.sem("s_w2c"); s_dm = p.sem("s_dm"); s_oa = p.sem("s_oa")
    s_rb = p.sem("s_rb"); s_rbe = p.sem("s_rbe"); s_g = p.sem("s_g")
    s_gst = [p.sem(f"s_gst1{i}") for i in range(2)]
    s_cst = p.sem("s_cst1")

    p.start()

    @p.block.sync
    def _(sync):
        sync.dma_start(out=xn_sb, in_=xnl.ap().rearrange("(dc pp) t -> pp dc t", pp=128)).then_inc(s_xl, 16)
        sync.dma_start(out=idb_sb[:], in_=idb.ap()).then_inc(s_cl, 16)
        sync.dma_start(out=idf_sb[:], in_=idf.ap()).then_inc(s_cl, 16)
        sync.wait_ge(s_rwte, 1)
        for k in range(2):
            sync.dma_start(out=rwd.ap()[k:k + 1, :].rearrange("o (i pp) -> (o i) pp", i=16),
                           in_=rwT_sb[k * 16:(k + 1) * 16, :]).then_inc(s_rwd, 16)
        sync.wait_ge(s_rwd, 32)
        for fc in range(FC):
            for tch in range(4):
                for k in range(2):
                    ft = fc * 4 + tch
                    sync.wait_ge(s_x2, 2 * ft + k + 1)
                    sync.dma_start(out=x2d[k].ap()[fc * 128:(fc + 1) * 128, tch * 512:(tch + 1) * 512],
                                   in_=x2t[k][:]).then_inc(s_x2st[k], 16)
        for tch in range(4):
            sync.wait_ge(s_rwd, 32)
            if tch >= 1:
                sync.wait_ge(s_oa, 16 * tch)
            sync.dma_start(out=rwrt[0][:], in_=rwd.ap()[0:1, tch * 512:(tch + 1) * 512]).then_inc(s_rwd, 16)
            sync.dma_start(out=rwrt[1][:], in_=rwd.ap()[1:2, tch * 512:(tch + 1) * 512]).then_inc(s_rwd, 16)
            for k in range(2):
                sync.wait_ge(s_x2st[k], 16 * 224)
                if tch == 0:
                    sync.wait_ge(s_x1, 896)
                else:
                    sync.wait_ge(s_dm, 8 * tch)
                    sync.wait_ge(s_a2mm, 56 * tch)
                sync.dma_start(out=x2r[k],
                               in_=x2d[k].ap()[:, tch * 512:(tch + 1) * 512]
                               .rearrange("(fc pp) t -> pp fc t", pp=128)).then_inc(s_x2l[k], 16)
            for gi in range(16):
                ti = tch * 16 + gi
                sync.wait_ge(s_g, ti + 1)
                sync.dma_start(out=g_fm.ap()[gi * 128:(gi + 1) * 128, tch * 512:(tch + 1) * 512],
                               in_=g_sb[ti % 2][:]).then_inc(s_gst[ti % 2], 16)
        sync.wait_ge(s_gst[0], 16 * 32)
        sync.wait_ge(s_gst[1], 16 * 32)

    @p.block.gpsimd
    def _(g):
        g.dma_start(out=wg_sb[:], in_=wgT.ap().rearrange("(dc pp) e -> pp dc e", pp=128)).then_inc(s_gl, 16)
        g.dma_start(out=aT_sb[:], in_=a1T.ap().rearrange("(dc pp) e -> pp dc e", pp=128)).then_inc(s_gl, 16)
        g.dma_start(out=b2_sb[:], in_=b2T.ap().rearrange("(erc pp) e -> pp erc e", pp=128)).then_inc(s_gl, 16)
        g.wait_ge(s_am, 8)
        g.dma_start(out=aT_sb[:], in_=a3T.ap().rearrange("(dc pp) e -> pp dc e", pp=128)).then_inc(s_a3l, 16)
        for fc in range(FC):
            sl = fc % 2
            if fc >= 2:
                g.wait_ge(s_bm, 8 * (fc - 1))
                g.wait_ge(s_x1, 16 * (fc - 1))
            g.dma_start(out=wst[sl][:], in_=w1T.ap()[:, fc * 128:(fc + 1) * 128]
                        .rearrange("(dc pp) f -> pp dc f", pp=128)).then_inc(s_ws[sl], 16)
            g.dma_start(out=wst[2 + sl][:], in_=w3T.ap()[:, fc * 128:(fc + 1) * 128]
                        .rearrange("(dc pp) f -> pp dc f", pp=128)).then_inc(s_ws[sl], 16)
            g.dma_start(out=bst[sl][:, 0:256], in_=b1T.ap()[:, fc * 128:(fc + 1) * 128]
                        .rearrange("(erc pp) f -> pp erc f", pp=128)).then_inc(s_bs[sl], 16)
            g.dma_start(out=bst[sl][:, 256:512], in_=b3T.ap()[:, fc * 128:(fc + 1) * 128]
                        .rearrange("(erc pp) f -> pp erc f", pp=128)).then_inc(s_bs[sl], 16)
            if fc >= 1:
                g.wait_ge(s_w2w[0], 16 * fc)
            g.dma_start(out=w2pc[:], in_=w2T.ap()[fc * 128:(fc + 1) * 128, :]).then_inc(s_w2p[0], 16)
            g.wait_ge(s_w2p[0], 16 * (fc + 1))
            g.dma_start(out=w2bf.ap()[fc * 128:(fc + 1) * 128, :], in_=w2pc[:]).then_inc(s_w2w[0], 16)
        for tch in range(4):
            for fc in range(FC):
                ai = tch * FC + fc
                sl = ai % 2
                if ai >= 2:
                    g.wait_ge(s_a2mm, ai - 1)
                g.dma_start(out=a2st[sl][:], in_=a2T.ap()[fc * 128:(fc + 1) * 128, :]).then_inc(s_a2l[sl], 16)
            for ot in range(8):
                for fc in range(FC):
                    wi = (tch * 8 + ot) * FC + fc
                    sl = wi % 2
                    if wi == 0:
                        g.wait_ge(s_w2w[0], 16 * FC)
                    if wi >= 2:
                        g.wait_ge(s_w2c, wi - 1)
                    g.dma_start(out=w2st[sl][:],
                                in_=bass.AP(w2bf, fc * 128 * 2048 + ot * 256, [[2048, 128], [1, 256]])
                                ).then_inc(s_w2l[sl], 16)

    @p.block.tensor
    def _(t):
        t.wait_ge(s_xl, 16)
        t.wait_ge(s_gl, 48)
        t.wait_ge(s_cst, 1)
        t.wait_ge(s_cl, 32)
        pbf = ps8[7].bitcast(BF16)
        ntp = 0
        def emit_tp_pair(tt, ik):
            nonlocal ntp
            tt.wait_ge(s_m256, ik + 1)
            for erc in range(2):
                if ntp >= 1:
                    tt.wait_ge(s_mf, ntp)
                tt.transpose(pbf[:, 0:128], m256[:, erc * 128:(erc + 1) * 128],
                             idb_sb[:]).then_inc(s_tp, 1)
                ntp += 1
        for i in range(16):
            if i >= 1:
                t.wait_ge(s_sx, i)
            for dc in range(16):
                mi = t.matmul(ps8[6][:, 0:8], xn_sb[:, dc * T + i * 128: dc * T + i * 128 + 128],
                              wg_sb[:, dc * 8:(dc + 1) * 8], start=(dc == 0), stop=(dc == 15))
                if dc == 15:
                    mi.then_inc(s_rt, 1)
            if i >= 1:
                emit_tp_pair(t, 2 * (i - 1))
                emit_tp_pair(t, 2 * (i - 1) + 1)
        emit_tp_pair(t, 30)
        emit_tp_pair(t, 31)
        t.wait_ge(s_pr, 16)
        t.wait_ge(s_mf, ntp)
        t.transpose(ps8[7][0:32, 0:128], rw_tok[:], idf_sb[:]).then_inc(s_rwt, 1)
        for w in range(2):
            if w == 1:
                t.wait_ge(s_a3l, 16)
            for erc in range(2):
                for tch in range(4):
                    gidx = (w * 2 + erc) * 4 + tch
                    if gidx >= 2:
                        t.wait_ge(s_ae, gidx - 1)
                    for dc in range(16):
                        mi = t.matmul(ps8[gidx % 2][:],
                                      aT_sb[:, dc * 256 + erc * 128: dc * 256 + (erc + 1) * 128],
                                      xn_sb[:, dc * T + tch * 512: dc * T + (tch + 1) * 512],
                                      start=(dc == 0), stop=(dc == 15))
                        if dc == 15:
                            mi.then_inc(s_am, 1)
        for fc in range(FC):
            sl = fc % 2
            t.wait_ge(s_ws[sl], 32 * (fc // 2 + 1))
            t.wait_ge(s_bs[sl], 32 * (fc // 2 + 1))
            for tch in range(4):
                ft = fc * 4 + tch
                if ft >= 1:
                    t.wait_ge(s_be, 2 * ft - 1)
                for dc in range(16):
                    mi = t.matmul(ps8[2][:], wst[sl][:, dc * 128:(dc + 1) * 128],
                                  xn_sb[:, dc * T + tch * 512: dc * T + (tch + 1) * 512],
                                  start=(dc == 0), stop=(dc == 15))
                    if dc == 15:
                        mi.then_inc(s_bm, 1)
                if ft >= 1:
                    t.wait_ge(s_be, 2 * ft)
                for dc in range(16):
                    mi = t.matmul(ps8[3][:], wst[2 + sl][:, dc * 128:(dc + 1) * 128],
                                  xn_sb[:, dc * T + tch * 512: dc * T + (tch + 1) * 512],
                                  start=(dc == 0), stop=(dc == 15))
                    if dc == 15:
                        mi.then_inc(s_bm, 1)
                for k in range(2):
                    t.wait_ge(s_be, 2 * ft + 1)
                    if 2 * ft + k >= 1:
                        t.wait_ge(s_sil, 2 * ft + k)
                    for erc in range(2):
                        t.matmul(ps8[4][:], bst[sl][:, 0 + erc * 128: 0 + (erc + 1) * 128],
                                 ma_sb[:, ((0 * 2 + k) * 2 + erc) * 2048 + tch * 512:
                                       ((0 * 2 + k) * 2 + erc) * 2048 + (tch + 1) * 512],
                                 start=(erc == 0), stop=False)
                    t.matmul(ps8[4][:], idb_sb[:], b1t[:], start=False, stop=True).then_inc(s_x1, 1)
                    t.wait_ge(s_be, 2 * ft + 2)
                    if 2 * ft + k >= 1:
                        t.wait_ge(s_x2, 2 * ft + k)
                    for erc in range(2):
                        t.matmul(ps8[5][:], bst[sl][:, 256 + erc * 128: 256 + (erc + 1) * 128],
                                 ma_sb[:, ((1 * 2 + k) * 2 + erc) * 2048 + tch * 512:
                                       ((1 * 2 + k) * 2 + erc) * 2048 + (tch + 1) * 512],
                                 start=(erc == 0), stop=False)
                    t.matmul(ps8[5][:], idb_sb[:], b3t[:], start=False, stop=True).then_inc(s_x1, 1)
        for tch in range(4):
            t.wait_ge(s_oa, 16 * tch)
            t.wait_ge(s_rwd, 32 + 32 * (tch + 1))
            for k in range(2):
                t.matmul(ps8[4 + k][:], ones_f32[:], rwrt[k][:],
                         start=True, stop=True).then_inc(s_rb, 1)
            for k in range(2):
                t.wait_ge(s_x2l[k], 16 * (tch + 1))
            t.wait_ge(s_a2e, 4 * tch)
            for fc in range(FC):
                ai = tch * FC + fc
                sl = ai % 2
                t.wait_ge(s_a2l[sl], 16 * (ai // 2 + 1))
                for erc in range(2):
                    for k in range(2):
                        gq = erc * 2 + k
                        mi = t.matmul(ps8[gq][:], a2st[sl][:, erc * 128:(erc + 1) * 128],
                                      x2r[k][:, fc * 512:(fc + 1) * 512],
                                      start=(fc == 0), stop=(fc == FC - 1))
                        if erc == 1 and k == 1:
                            mi.then_inc(s_a2mm, 1)
                        if fc == FC - 1 and erc == 1 and k == 1:
                            mi.then_inc(s_a2m, 1) if False else None
            for ot in range(8):
                t.wait_ge(s_oa, 16 * tch + 2 * ot)
                t.wait_ge(s_rbe, 2 * (tch + 1))
                if tch == 0 and ot == 0:
                    t.wait_ge(s_a2e, 4)
                for fc in range(FC):
                    wi = (tch * 8 + ot) * FC + fc
                    sl = wi % 2
                    t.wait_ge(s_w2l[sl], 16 * (wi // 2 + 1))
                    for hcl in range(2):
                        for k in range(2):
                            gq = 4 + hcl * 2 + k
                            mi = t.matmul(ps8[gq][:], w2st[sl][:, hcl * 128:(hcl + 1) * 128],
                                          x2r[k][:, fc * 512:(fc + 1) * 512],
                                          start=(fc == 0), stop=False)
                            if hcl == 1 and k == 1:
                                mi.then_inc(s_w2c, 1)
                t.wait_ge(s_a2e, 4 * (tch + 1))
                hc0 = ot * 2
                for hcl in range(2):
                    for k in range(2):
                        gq = 4 + hcl * 2 + k
                        for erc in range(2):
                            mi = t.matmul(ps8[gq][:],
                                          b2_sb[:, erc * 2048 + (hc0 + hcl) * 128: erc * 2048 + (hc0 + hcl) * 128 + 128],
                                          ma2[:, (k * 2 + erc) * 512:(k * 2 + erc + 1) * 512],
                                          start=False, stop=(erc == 1))
                        if hcl == 1 and k == 1:
                            mi.then_inc(s_dm, 1)

    @p.block.scalar
    def _(sc):
        for i in range(16):
            sc.wait_ge(s_rt, i + 1)
            sc.wait_ge(s_sm, i + 1)
            sc.activation(exp_sb[:], ps8[6][:, 0:8], AF.Exp, bias=srow[:, 1:2], scale=1.0,
                          accum_out=srow[:, 2:3]).then_inc(s_sx, 1)
        for fc in range(FC):
            for tch in range(4):
                ft = fc * 4 + tch
                sc.wait_ge(s_bm, 2 * ft + 1)
                sc.copy(b1t[:], ps8[2][:]).then_inc(s_be, 1)
                sc.wait_ge(s_bm, 2 * ft + 2)
                sc.copy(b3t[:], ps8[3][:]).then_inc(s_be, 1)
                for k in range(2):
                    sc.wait_ge(s_x1, 4 * ft + 2 * k + 1)
                    if 2 * ft + k >= 1:
                        sc.wait_ge(s_x2, 2 * ft + k)
                    sc.activation(x1s[:], ps8[4][:], AF.Silu).then_inc(s_sil, 1)
        for tch in range(4):
            for k in range(2):
                sc.wait_ge(s_rb, 2 * tch + k + 1)
                sc.copy(rwb2[k][:], ps8[4 + k][:]).then_inc(s_rbe, 1)
            for gi in range(16):
                ti = tch * 16 + gi
                sc.wait_ge(s_oa, ti + 1)
                if ti >= 2:
                    sc.wait_ge(s_gst[ti % 2], 16 * (ti // 2))
                sc.activation(g_sb[ti % 2][:], out2[ti % 2][:], AF.Silu).then_inc(s_g, 1)

    @p.block.vector
    def _(v):
        v.memset(rep1[:], 2.0)
        v.memset(rep2[:], 2.0)
        v.memset(ones_f32[:], 1.0).then_inc(s_cst, 1)
        for i in range(16):
            v.wait_ge(s_rt, i + 1)
            p.guard(v, v.reduce_max(srow[:, 0:1], ps8[6][:, 0:8], axis=AX.X))
            v.tensor_scalar_mul(srow[:, 1:2], srow[:, 0:1], -1.0).then_inc(s_sm, 1)
            v.wait_ge(s_sx, i + 1)
            p.guard(v, v.reciprocal(srow[:, 3:4], srow[:, 2:3]))
            p.guard(v, v.tensor_scalar_mul(probs[:, i * 8:(i + 1) * 8], exp_sb[:], srow[:, 3:4]))
            p.guard(v, v.max(out=m8t[:], in_=probs[:, i * 8:(i + 1) * 8]))
            p.guard(v, v.tensor_copy(rep1[:, 0:1], m8t[:, 0:1]))
            p.guard(v, v.tensor_copy(rep2[:, 0:1], m8t[:, 1:2]))
            p.guard(v, v.match_replace(out=scr1[:], in_to_replace=rep1[:],
                                       in_values=probs[:, i * 8:(i + 1) * 8], imm_value=-1.0))
            p.guard(v, v.tensor_tensor(mk1[:], scr1[:], probs[:, i * 8:(i + 1) * 8], ALU.not_equal))
            p.guard(v, v.match_replace(out=scr2[:], in_to_replace=rep2[:],
                                       in_values=scr1[:], imm_value=-1.0))
            p.guard(v, v.tensor_tensor(mk2[:], scr2[:], scr1[:], ALU.not_equal))
            p.guard(v, v.tensor_add(srow[:, 4:5], m8t[:, 0:1], m8t[:, 1:2]))
            p.guard(v, v.reciprocal(srow[:, 5:6], srow[:, 4:5]))
            p.guard(v, v.tensor_tensor(rw_tok[:, i:i + 1], m8t[:, 0:1], srow[:, 5:6], ALU.mult))
            gi2 = v.tensor_tensor(rw_tok[:, 16 + i:16 + i + 1], m8t[:, 1:2], srow[:, 5:6], ALU.mult)
            gi2.then_inc(s_pr, 1)
            p.guard(v, gi2) if False else None
            for k in range(2):
                ik = i * 2 + k
                mkk = mk1 if k == 0 else mk2
                if ik >= 1:
                    pi, pk = (ik - 1) // 2, (ik - 1) % 2
                    for erc in range(2):
                        v.wait_ge(s_tp, 2 * (ik - 1) + erc + 1)
                        v.tensor_copy(maskf[:, (pk * 2 + erc) * T + pi * 128: (pk * 2 + erc) * T + pi * 128 + 128],
                                      ps8[7].bitcast(BF16)[:, 0:128]).then_inc(s_mf, 1)
                src_ap = bass.AP(mkk.ap().tensor, 0, [[8, 128], [1, 8], [0, 32]])
                v.tensor_copy(m256.ap().rearrange("pp (e r) -> pp e r", e=8), src_ap).then_inc(s_m256, 1)
        for erc in range(2):
            v.wait_ge(s_tp, 62 + erc + 1)
            v.tensor_copy(maskf[:, (1 * 2 + erc) * T + 15 * 128: (1 * 2 + erc) * T + 15 * 128 + 128],
                          ps8[7].bitcast(BF16)[:, 0:128]).then_inc(s_mf, 1)
        v.wait_ge(s_rwt, 1)
        v.tensor_copy(rwT_sb[:], ps8[7][0:32, 0:128]).then_inc(s_rwte, 1)
        v.wait_ge(s_mf, 64)
        for w in range(2):
            for erc in range(2):
                for tch in range(4):
                    gidx = (w * 2 + erc) * 4 + tch
                    v.wait_ge(s_am, gidx + 1)
                    for k in range(2):
                        ii = v.tensor_tensor(
                            ma_sb[:, ((w * 2 + k) * 2 + erc) * 2048 + tch * 512:
                                  ((w * 2 + k) * 2 + erc) * 2048 + (tch + 1) * 512],
                            ps8[gidx % 2][:],
                            maskf[:, (k * 2 + erc) * T + tch * 512: (k * 2 + erc) * T + (tch + 1) * 512],
                            ALU.mult)
                        if k == 1:
                            ii.then_inc(s_ae, 1)
        for fc in range(FC):
            for tch in range(4):
                ft = fc * 4 + tch
                for k in range(2):
                    v.wait_ge(s_sil, 2 * ft + k + 1)
                    v.wait_ge(s_x1, 4 * ft + 2 * k + 2)
                    v.wait_ge(s_x2st[k], 16 * ft)
                    v.tensor_tensor(x2t[k][:], x1s[:], ps8[5][:], ALU.mult).then_inc(s_x2, 1)
        for tch in range(4):
            v.wait_ge(s_a2mm, tch * FC + FC)
            for gq in range(4):
                erc, k = gq // 2, gq % 2
                ii = v.tensor_tensor(ma2[:, (k * 2 + erc) * 512:(k * 2 + erc + 1) * 512],
                                     ps8[gq][:],
                                     maskf[:, (k * 2 + erc) * T + tch * 512: (k * 2 + erc) * T + (tch + 1) * 512],
                                     ALU.mult)
                ii.then_inc(s_a2e, 1)
            for ot in range(8):
                v.wait_ge(s_dm, tch * 8 + ot + 1)
                v.wait_ge(s_rbe, 2 * (tch + 1))
                for hcl in range(2):
                    ti = tch * 16 + ot * 2 + hcl
                    if ti >= 2:
                        v.wait_ge(s_g, ti - 1)
                    p.guard(v, v.tensor_tensor(out2[ti % 2][:], ps8[4 + hcl * 2][:], rwb2[0][:], ALU.mult))
                    p.guard(v, v.tensor_tensor(tmp_sb[:], ps8[4 + hcl * 2 + 1][:], rwb2[1][:], ALU.mult))
                    v.tensor_tensor(out2[ti % 2][:], out2[ti % 2][:], tmp_sb[:], ALU.add).then_inc(s_oa, 1)

    p.close()
    return p.nc


def kernel(**inputs):
    maps = stage_inputs(inputs)
    CORES = list(range(8))

    def run(buildfn, outs, *args, remap=None):
        nc = buildfn(*args)
        res = run_bass_kernel_spmd(nc, filter_map(maps, nc), core_ids=CORES)
        for c in CORES:
            for k in outs:
                maps[c][(remap or {}).get(k, k)] = np.asarray(res.results[c][k])
        return res

    run(build_ln, ["xn_full", "xn_loc"])
    run(build_qkv, ["q_fm", "k_fm", "v_blk"], 0)
    run(build_attn, ["av_full"], 0)
    run(build_ffn0, ["g_fm"])
    run(build_ep, ["x_out"], 0, remap={"x_out": "x_in"})
    run(build_ln, ["xn_full", "xn_loc"])
    run(build_qkv, ["q_fm", "k_fm", "v_blk"], 1)
    run(build_rope, ["q_ro", "k_ro"])
    run(build_attn, ["av_full"], 1)
    run(build_ffn1, ["g_fm"])
    res = run(build_ep, ["x_out"], 1)

    out = np.zeros((2, 8192, 2048), np.float32)
    for c in CORES:
        b, j = c // 4, c % 4
        got = np.asarray(res.results[c]["x_out"], np.float32)
        out[b, j * 1024:(j + 1) * 1024] = got[0:1024]
        out[b, (7 - j) * 1024:(8 - j) * 1024] = got[1024:2048]
    return out
